# version 31
# baseline (speedup 1.0000x reference)
import contextlib
import math

import numpy as np

import concourse.bass as bass
import concourse.mybir as mybir
from concourse.bass_utils import run_bass_kernel_spmd

F32 = mybir.dt.float32
BF16 = mybir.dt.bfloat16
Alu = mybir.AluOpType
Act = mybir.ActivationFunctionType
AX = mybir.AxisListType

NCORES = 8
SEQ = 2
N = 2048
D = 1024
H2 = 4096
DV = 2048
QK = 128
HF = 4096
NT = N // 128
TT = SEQ * NT

C32 = float(1.5 * 2.0**23)
C52 = float(1.5 * 2.0**52)
LN2_HI = float(np.float32(math.log(2.0)))
INV_LN2 = float(np.float32(1.0 / math.log(2.0)))
BETA = float(np.float32(math.log(2.0) / LN2_HI - 1.0))
BIG = 3.0e38

TAPS = False


class Sem:
    def __init__(self, es, nc, name):
        self.h = es.enter_context(nc.semaphore(name))
        self.count = 0

    def inc(self, instr, n=1):
        instr.then_inc(self.h, n)
        self.count += n

    def dma(self, instr):
        instr.then_inc(self.h, 16)
        self.count += 16


_WRAP_OPS = {
    "tensor_scalar", "tensor_tensor", "tensor_copy", "tensor_add", "tensor_sub",
    "tensor_mul", "tensor_max", "reduce_sum", "reduce_max", "tensor_reduce",
    "scalar_tensor_tensor", "reciprocal", "memset", "memzero", "iota",
    "affine_select", "engine_nop", "bn_stats", "bn_aggr", "select", "tensor_relu",
    "activation", "copy", "mul", "add", "square", "sign", "sqrt",
}


class SE:

    def __init__(self, eng, sem):
        self._e = eng
        self._s = sem

    def __getattr__(self, n):
        f = getattr(self._e, n)
        if n in _WRAP_OPS:
            def wrapped(*a, **k):
                self._e.wait_ge(self._s.h, self._s.count)
                ins = f(*a, **k)
                self._s.inc(ins)
                return ins
            return wrapped
        if n == "dma_start":
            def wrapped(*a, **k):
                self._e.wait_ge(self._s.h, self._s.count)
                return f(*a, **k)
            return wrapped
        return f


def build_program():
    nc = bass.Bass("TRN2", target_bir_lowering=False)
    es = contextlib.ExitStack()

    x_h = nc.dram_tensor("x", [SEQ, N, D], F32, kind="ExternalInput")
    out_h = nc.dram_tensor("out", [SEQ, N, D], F32, kind="ExternalOutput")
    wh_h = nc.dram_tensor("L0_attn_W_hidden", [D, H2], F32, kind="ExternalInput")
    bh_h = nc.dram_tensor("L0_attn_b_hidden", [H2], F32, kind="ExternalInput")
    wqk_h = nc.dram_tensor("L0_attn_W_qk", [D, QK], F32, kind="ExternalInput")
    bqk_h = nc.dram_tensor("L0_attn_b_qk", [QK], F32, kind="ExternalInput")
    osg_h = nc.dram_tensor("L0_attn_os_gamma", [2, QK], F32, kind="ExternalInput")
    osb_h = nc.dram_tensor("L0_attn_os_beta", [2, QK], F32, kind="ExternalInput")
    wo_h = nc.dram_tensor("L0_attn_W_out", [DV, D], F32, kind="ExternalInput")
    bo_h = nc.dram_tensor("L0_attn_b_out", [D], F32, kind="ExternalInput")
    lng_h = nc.dram_tensor("L0_attn_ln_g", [D], F32, kind="ExternalInput")
    lnb_h = nc.dram_tensor("L0_attn_ln_b", [D], F32, kind="ExternalInput")
    w1_h = nc.dram_tensor("L0_ffn_w1", [HF, D], F32, kind="ExternalInput")
    b1_h = nc.dram_tensor("L0_ffn_b1", [HF], F32, kind="ExternalInput")
    w2_h = nc.dram_tensor("L0_ffn_w2", [D, HF], F32, kind="ExternalInput")
    b2_h = nc.dram_tensor("L0_ffn_b2", [D], F32, kind="ExternalInput")

    sk = "ExternalOutput" if TAPS else "Internal"
    sc_vh = nc.dram_tensor("sc_vh", [SEQ, N, DV], BF16, kind=sk)
    sc_vl = nc.dram_tensor("sc_vl", [SEQ, N, DV], BF16, kind=sk)
    sc_g = nc.dram_tensor("sc_g", [SEQ, N, DV], F32, kind=sk)
    sc_qh = nc.dram_tensor("sc_qh", [SEQ, QK, N], BF16, kind=sk)
    sc_ql = nc.dram_tensor("sc_ql", [SEQ, QK, N], BF16, kind=sk)
    sc_kh = nc.dram_tensor("sc_kh", [SEQ, QK, N], BF16, kind=sk)
    sc_kl = nc.dram_tensor("sc_kl", [SEQ, QK, N], BF16, kind=sk)
    sc_ua = nc.dram_tensor("sc_ua", [SEQ, N, DV], F32, kind=sk)
    sc_x2 = nc.dram_tensor("sc_x2", [SEQ, N, D], F32, kind=sk)
    sc_hq = nc.dram_tensor("sc_hq", [TT, HF // 128, 128, 128], BF16, kind=sk)
    sc_wq2 = nc.dram_tensor("sc_wq2", [D, HF], BF16, kind=sk)
    sc_row = nc.dram_tensor("sc_row", [2, H2], F32, kind="Internal")
    sc_xb = nc.dram_tensor("sc_xb", [8, 2, 128, 4096], BF16, kind="Internal")

    def S(name):
        return Sem(es, nc, name)
    s_mm, s_v, s_s, s_g = S("s_mm"), S("s_v"), S("s_s"), S("s_g")
    s_in = S("s_in")
    s_bias = S("s_bias")
    s_row0, s_row1 = S("s_row0"), S("s_row1")
    s_xa = [S("s_xa0"), S("s_xa1")]
    s_qts = S("s_qts")
    s_ev = [S(f"s_ev{i}") for i in range(4)]
    s_bqk = S("s_bqk")
    s_vb = [S("s_vb0"), S("s_vb1")]
    s_vg = [S("s_vg0"), S("s_vg1")]
    s_c = [S("s_c0"), S("s_c1")]
    s_wo = S("s_wo")
    s_x2sp = S("s_x2sp")
    s_wc = S("s_wc")
    s_wb2 = S("s_wb2")
    s_w1 = [S("s_w1a"), S("s_w1b")]
    s_xd1 = S("s_xd1")
    s_xd1b = S("s_xd1b")
    s_wq = S("s_wq")
    s_d2 = S("s_d2")
    s_out = S("s_out")
    s_hq = [S("s_hq0"), S("s_hq1")]
    s_x2b = S("s_x2b")
    s_wcg = S("s_wcg")
    s_ws = [S("s_ws0"), S("s_ws1")]
    s_xb = [S("s_xba"), S("s_xbb")]
    ALL_SEMS = [s_mm, s_v, s_s, s_g, s_in, s_bias, s_row0, s_row1, *s_xa, s_qts,
                *s_ev, s_bqk, *s_vb, *s_vg, *s_c, s_wo, s_x2sp, s_wc, s_wb2, *s_w1,
                s_xd1, s_xd1b, s_wq, s_d2, s_out, *s_hq, s_x2b,
                s_wcg, *s_ws, *s_xb]

    sb = lambda name, shape, dt=F32: es.enter_context(nc.sbuf_tensor(name, shape, dt))
    ident = sb("ident", [128, 128])
    eps_t = sb("eps_t", [128, 1])
    rsc2 = sb("rsc2", [128, NT])
    rmask = sb("rmask", [128, 896], BF16)
    g0 = sb("g0", [128, 1]); be0 = sb("be0", [128, 1])
    g1 = sb("g1", [128, 1]); be1 = sb("be1", [128, 1])
    bqk_eff = sb("bqk_eff", [128, 1])
    lnb_c = sb("lnb_c", [128, 8])
    lng_c = sb("lng_c", [128, 8])
    st1 = sb("st1", [128, 1]); st2 = sb("st2", [128, 1])
    st3 = sb("st3", [128, 1]); st4 = sb("st4", [128, 1])
    qkt = sb("qkt", [128, 128])
    qt = sb("qt", [128, 128]); kt = sb("kt", [128, 128])
    ev = [sb(f"ev{i}", [128, 512]) for i in range(4)]
    b1tab = sb("b1tab", [128, HF // 128])
    bqt = sb("bqt", [128, D])
    qreg = sb("qreg", [128, 8])
    q_one, q_wm, q_iw, q_cve4, q_cvef, q_cb, q_ncb, q_tmp = \
        [qreg[:, i:i + 1] for i in range(8)]

    wba = sb("wba", [128, 8 * H2], BF16)
    wbb = sb("wbb", [128, 8 * H2], BF16)
    wmid = sb("wmid", [128, 12288])
    wqkh = sb("wqkh", [128, 8 * 128], BF16)
    wqkl = sb("wqkl", [128, 8 * 128], BF16)
    zbh = [sb("zbh0", [128, D], BF16), sb("zbh1", [128, D], BF16)]
    zbl = [sb("zbl0", [128, D], BF16), sb("zbl1", [128, D], BF16)]
    qth = sb("qth", [128, 128], BF16); qtl = sb("qtl", [128, 128], BF16)
    kth = sb("kth", [128, 128], BF16); ktl = sb("ktl", [128, 128], BF16)
    vsph = sb("vsph", [128, 512], BF16)
    vspl = sb("vspl", [128, 512], BF16)

    wstg = [wmid[:, 0:2048], wmid[:, 2048:4096]]
    wqks = wmid[:, 4096:5120]
    rowbuf2 = wmid[0:1, 5120:9216]
    lnbs = wmid[0:8, 9216:9344]
    lngs = wmid[0:8, 9344:9472]
    b1s = wmid[0:32, 9472:9600]
    rowbuf = wmid[0:1, 0:H2]
    biash = wmid[:, 0:4096]
    xin = wmid[:, 4096:6144]
    xmt_a = wmid[:, 6144:7168]
    zt_a = wmid[:, 7168:8192]
    qsrc = [wmid[:, 0:1024], wmid[:, 1024:2048]]
    qd_t0 = wmid[:, 2048:3072]
    qd_t3 = wmid[:, 3072:4096]
    qd_t12 = [(wmid[:, 4096:5120], wmid[:, 5120:6144]),
              (wmid[:, 6144:7168], wmid[:, 7168:8192])]
    xmt_d1 = [wmid[:, 10240:11264], wmid[:, 11264:12288]]
    w1st = [wmid[:, 8192:9216], wmid[:, 9216:10240]]
    x2h = [wba[:, 16384:20480], wba[:, 20480:24576]]
    x2l = [wbb[:, 16384:20480], wbb[:, 20480:24576]]
    hqb = [zbh[0], zbl[0]]
    hqld = [wbb[:, 0:4096], wbb[:, 4096:8192]]
    obuf_db = [wmid[:, 0:1024], wmid[:, 1024:2048]]
    oqbuf_db = [wmid[:, 2048:3072], wmid[:, 3072:4096]]
    u3_db = [wmid[:, 4096:5120], wmid[:, 5120:6144]]
    q2_t0 = wmid[:, 6144:7168]
    q2_t3 = wmid[:, 7168:8192]
    q2_t12 = [(wmid[:, 8192:9216], wmid[:, 9216:10240]),
              (wmid[:, 10240:11264], wmid[:, 11264:12288])]
    lnsq_d2 = wmid[:, 6144:7168]
    w2stB = wmid[:, 8192:10240]
    qt_wB = [wmid[:, 10240 + 512 * i:10240 + 512 * (i + 1)] for i in range(3)]
    wqb = [wba[:, 28672:30720], wba[:, 30720:32768]]
    qfh, kfh = wba[:, 0:N], wba[:, N:2 * N]
    qfl, kfl = wbb[:, 0:N], wbb[:, N:2 * N]
    aT = [wmid[:, 512 * j:512 * (j + 1)] for j in range(16)]
    aTh = [wba[:, 4096 + 512 * j:4096 + 512 * (j + 1)] for j in range(16)]
    aTl = [wbb[:, 4096 + 512 * j:4096 + 512 * (j + 1)] for j in range(16)]
    vgh = [wba[:, 12288:20480], wba[:, 20480:28672]]
    vgl = [wbb[:, 12288:20480], wbb[:, 20480:28672]]
    woh = wba[:, 0:16384]
    wol = wbb[:, 0:16384]
    oghm = [wba[:, 16384:18432], wba[:, 18432:20480]]
    oglm = [wbb[:, 16384:18432], wbb[:, 18432:20480]]
    ua_db = [wmid[:, 0:2048], wmid[:, 2048:4096]]
    gt_db = [wmid[:, 4096:6144], wmid[:, 6144:8192]]
    x_db = [wmid[:, 8192:9216], wmid[:, 9216:10240]]
    og_t = wmid[:, 10240:12288]
    u_t_c = og_t[:, 1024:2048]
    x2t_db = [gt_db[0][:, 0:1024], gt_db[1][:, 0:1024]]
    sq_db = [gt_db[0][:, 1024:2048], gt_db[1][:, 1024:2048]]
    wostg = [wmid[:, 0:1024], wmid[:, 1024:2048]]

    pall = es.enter_context(nc.psum_tensor("pall", [128, 4096], F32))
    PB = [pall[:, 512 * b:512 * (b + 1)] for b in range(8)]

    ps = {}

    with nc.Block() as block:
        free_at = [None] * 8
        rr = [0]

        def get_bank(te):
            b = rr[0]
            rr[0] = (rr[0] + 1) % 8
            if free_at[b] is not None:
                sem, cnt = free_at[b]
                te.wait_ge(sem.h, cnt)
            return b

        def bank_done(b, sem):
            free_at[b] = (sem, sem.count)

        def barrier():
            for opener in (block.sync, block.scalar, block.vector, block.tensor, block.gpsimd):
                def bodyf(e):
                    for sm in ALL_SEMS:
                        if sm.count > 0:
                            e.wait_ge(sm.h, sm.count)
                opener(bodyf)

        def VB(f):
            block.vector(lambda e: f(SE(e, s_v)))

        def SCB(f):
            block.scalar(lambda e: f(SE(e, s_s)))

        def GB(f):
            block.gpsimd(lambda e: f(SE(e, s_g)))

        def body(sync):
            sync.dma_start(out=lnbs, in_=bass.AP(lnb_h, 0, [[128, 8], [1, 128]])).then_inc(s_in.h, 16)
            sync.dma_start(out=lngs, in_=bass.AP(lng_h, 0, [[128, 8], [1, 128]])).then_inc(s_in.h, 16)
            sync.dma_start(out=b1s, in_=bass.AP(b1_h, 0, [[128, 32], [1, 128]])).then_inc(s_in.h, 16)
            sync.dma_start(out=g0[:], in_=bass.AP(osg_h, 0, [[1, 128], [1, 1]])).then_inc(s_in.h, 16)
            sync.dma_start(out=g1[:], in_=bass.AP(osg_h, 128, [[1, 128], [1, 1]])).then_inc(s_in.h, 16)
            sync.dma_start(out=be0[:], in_=bass.AP(osb_h, 0, [[1, 128], [1, 1]])).then_inc(s_in.h, 16)
            sync.dma_start(out=be1[:], in_=bass.AP(osb_h, 128, [[1, 128], [1, 1]])).then_inc(s_in.h, 16)
            sync.dma_start(out=bqk_eff[:], in_=bass.AP(bqk_h, 0, [[1, 128], [1, 1]])).then_inc(s_in.h, 16)
            sync.dma_start(out=rowbuf2, in_=bass.AP(bh_h, 0, [[H2, 1], [1, H2]])).then_inc(s_in.h, 16)
            for c in range(8):
                sync.dma_start(out=wqks[:, c * 128:(c + 1) * 128],
                               in_=wqk_h.ap()[c * 128:(c + 1) * 128, :]).then_inc(s_in.h, 16)
        block.sync(body)
        s_in.count += 16 * 17
        pre_loads = s_in.count

        def body(g):
            g.memset(eps_t[:], 1e-5)
            g.memset(ev[0][:, 0:448], 1.0)
            g.affine_select(rmask[:, 0:448], ev[0][:, 0:448], [[1, 448]], Alu.is_ge, 0.0,
                            base=-384, channel_multiplier=-1)
            g.memset(ev[0][:, 0:448], 1.0)
            g.affine_select(rmask[:, 448:896], ev[0][:, 0:448], [[1, 448]], Alu.is_ge, 0.0,
                            base=64, channel_multiplier=-1)
            g.memset(ev[1][:, 0:128], 1.0)
            g.affine_select(ident[:], ev[1][:, 0:128], [[1, 128]], Alu.is_equal, 0.0,
                            base=0, channel_multiplier=-1)
            g.iota(rsc2[:], [[128, NT]], base=1, channel_multiplier=1,
                   allow_small_or_imprecise_dtypes=True)
            g.memset(q_one, 1.0)
        GB(body)
        g_pre = s_g.count

        def body(v):
            v.wait_ge(s_g.h, g_pre)
            v.reciprocal(ev[2][:, 0:NT], rsc2[:])
            v.tensor_tensor(rsc2[:], ev[2][:, 0:NT], ev[2][:, 0:NT], Alu.mult)
            v.tensor_scalar(q_wm, q_one, C52, -C52, Alu.add, Alu.add)
            v.tensor_scalar(q_iw, q_wm, -1.0, 1.0, Alu.mult, Alu.add)
            v.tensor_scalar(q_cve4, q_iw, 2.0**20, None, Alu.mult)
            v.tensor_scalar(q_tmp, q_wm, 2.0**49, 1.0, Alu.mult, Alu.add)
            v.tensor_tensor(q_cve4, q_cve4, q_tmp, Alu.add)
            v.tensor_scalar(q_cvef, q_iw, 2.0**13, None, Alu.mult)
            v.tensor_scalar(q_tmp, q_wm, 2.0**42, 1.0, Alu.mult, Alu.add)
            v.tensor_tensor(q_cvef, q_cvef, q_tmp, Alu.add)
            v.tensor_scalar(q_cb, q_iw, C32, None, Alu.mult)
            v.tensor_scalar(q_tmp, q_wm, C52, None, Alu.mult)
            v.tensor_tensor(q_cb, q_cb, q_tmp, Alu.add)
            v.tensor_scalar(q_ncb, q_cb, -1.0, None, Alu.mult)
        VB(body)

        def body(t):
            t.wait_ge(s_in.h, pre_loads)
            t.wait_ge(s_g.h, g_pre)
            t.transpose(pall[:, 0:8], lnbs, ident[0:8, 0:8])
            t.transpose(pall[:, 8:16], lngs, ident[0:8, 0:8])
            ins = t.transpose(pall[:, 16:48], b1s, ident[0:32, 0:32])
            s_mm.inc(ins)
        block.tensor(body)
        tp3 = s_mm.count

        def body(v):
            v.wait_ge(s_mm.h, tp3)
            v.tensor_copy(lnb_c[:], pall[:, 0:8])
            v.tensor_copy(lng_c[:], pall[:, 8:16])
            v.tensor_copy(b1tab[:], pall[:, 16:48])
        VB(body)
        lnbc_ready = s_v.count

        def body(t):
            t.wait_ge(s_in.h, pre_loads)
            t.wait_ge(s_v.h, lnbc_ready)
            for c in range(8):
                ins = t.matmul(pall[:, 0:1], wqks[:, 128 * c:128 * (c + 1)], lnb_c[:, c:c + 1],
                               start=(c == 0), stop=(c == 7))
            s_mm.inc(ins)
        block.tensor(body)
        bqk_mm = s_mm.count

        def body(v):
            v.wait_ge(s_mm.h, bqk_mm)
            v.tensor_tensor(bqk_eff[:], bqk_eff[:], pall[:, 0:1], Alu.add)
            for c in range(8):
                v.tensor_scalar(wqks[:, c * 128:(c + 1) * 128], wqks[:, c * 128:(c + 1) * 128],
                                lng_c[:, c:c + 1], None, Alu.mult)
            v.tensor_copy(wqkh[:], wqks[:])
            v.tensor_tensor(wqkl[:], wqks[:], wqkh[:], Alu.subtract)
        VB(body)
        bqk_done = s_v.count

        wh_split = {}
        for ch in range(16):
            c, half = divmod(ch, 2)
            stg = wstg[ch % 2]
            sw = s_w1[ch % 2]

            def body(sync, c=c, half=half, ch=ch, stg=stg, sw=sw):
                if ch >= 2:
                    sync.wait_ge(s_v.h, wh_split[ch - 2])
                ins = sync.dma_start(out=stg, in_=wh_h.ap()[c * 128:(c + 1) * 128,
                                                            2048 * half:2048 * (half + 1)])
                sw.dma(ins)
            block.sync(body)
            wh_loaded = sw.count

            def body(t, c=c, half=half, stg=stg, sw=sw, wh_loaded=wh_loaded):
                t.wait_ge(sw.h, wh_loaded)
                if c == 0 and half == 0:
                    t.wait_ge(s_v.h, bqk_done)
                for bb in range(4):
                    ins = t.matmul(pall[0:1, 2048 * half + 512 * bb:2048 * half + 512 * (bb + 1)],
                                   lnb_c[:, c:c + 1], stg[:, 512 * bb:512 * (bb + 1)],
                                   start=(c == 0), stop=(c == 7))
                s_mm.inc(ins)
            block.tensor(body)
            wh_mm = s_mm.count

            def body(v, c=c, half=half, stg=stg, wh_mm=wh_mm):
                v.wait_ge(s_mm.h, wh_mm)
                v.tensor_scalar(stg, stg, lng_c[:, c:c + 1], None, Alu.mult)
                dst = slice(c * H2 + 2048 * half, c * H2 + 2048 * (half + 1))
                v.tensor_copy(wba[:, dst], stg)
                v.tensor_tensor(wbb[:, dst], stg, wba[:, dst], Alu.subtract)
            VB(body)
            wh_split[ch] = s_v.count
        a_wready = s_v.count

        def body(v):
            v.wait_ge(s_mm.h, s_mm.count)
            v.wait_ge(s_in.h, pre_loads)
            v.tensor_tensor(rowbuf2, pall[0:1, :], rowbuf2, Alu.add)
        VB(body)
        rowadd_v = s_v.count
        for b in range(8):
            free_at[b] = (s_v, s_v.count)

        def body(s_):
            s_.wait_ge(s_v.h, rowadd_v)
            ins = s_.dma_start(out=sc_row.ap()[0:1, :], in_=rowbuf2)
            s_row0.dma(ins)
        block.scalar(body)

        def body(sync):
            sync.wait_ge(s_row0.h, s_row0.count)
            sync.wait_ge(s_v.h, a_wready)
            ins = sync.dma_start(out=biash, in_=bass.AP(sc_row, 0, [[0, 128], [1, H2]]))
            s_bias.dma(ins)
        block.sync(body)

        def ln_tile(width, src_ap, dst_ap, xm, sq, rsqrt_scalar=False):
            def body(v):
                v.reduce_sum(st1[:], src_ap, axis=AX.X)
                v.tensor_scalar(st2[:], st1[:], 1.0 / width, None, Alu.mult)
                v.tensor_scalar(xm, src_ap, st2[:], None, Alu.subtract)
                v.scalar_tensor_tensor(sq, xm, 1.0, xm, Alu.mult, Alu.mult,
                                       accum_out=st3[:])
            VB(body)

            if rsqrt_scalar:
                def body(s_):
                    s_.wait_ge(s_v.h, s_v.count)
                    s_.activation(st4[:], st3[:], Act.Ln, bias=eps_t[:], scale=1.0 / width)
                    s_.activation(st4[:], st4[:], Act.Exp, scale=-0.5)
                SCB(body)

                def body(v):
                    v.wait_ge(s_s.h, s_s.count)
                    v.tensor_scalar(dst_ap, xm, st4[:], None, Alu.mult)
                VB(body)
                return

            def body(s_):
                s_.wait_ge(s_v.h, s_v.count)
                s_.activation(st4[:], st3[:], Act.Sqrt, bias=eps_t[:], scale=1.0 / width)
            SCB(body)

            def body(v):
                v.wait_ge(s_s.h, s_s.count)
                v.reciprocal(st1[:], st4[:])
                v.tensor_scalar(dst_ap, xm, st1[:], None, Alu.mult)
            VB(body)

        def emit_quant2(src, dst, t0, t1, t2, t3, mode, gates=(), ggates=()):
            if mode == "e4m3":
                cve, SSC, SISC, THRSQ, OVT2 = q_cve4, 512.0, 2.0**-9, 2.0**-12, 61000.0
            else:
                cve, SSC, SISC, THRSQ, OVT2 = q_cvef, 2.0**24, 2.0**-24, 2.0**-28, 4292870144.0

            def gbody(g):
                for sem, cnt in ggates:
                    g.wait_ge(sem.h, cnt)
                for sem, cnt in gates:
                    g.wait_ge(sem.h, cnt)
                g.tensor_scalar(t1, src, SSC, q_cb[:], Alu.mult, Alu.add)
                g.tensor_scalar(t1, t1, q_ncb[:], SISC, Alu.add, Alu.mult)
                g.tensor_tensor(t2, src, src, Alu.mult)
                g.tensor_scalar(t2, t2, THRSQ, None, Alu.is_lt)
            GB(gbody)
            gp_done = s_g.count

            def body(v):
                for sem, cnt in gates:
                    v.wait_ge(sem.h, cnt)
                v.tensor_scalar(t0, src, cve, None, Alu.mult)
                v.tensor_tensor(t3, t0, src, Alu.subtract)
                v.tensor_tensor(t0, t0, t3, Alu.subtract)
                v.wait_ge(s_g.h, gp_done)
                v.tensor_tensor(t3, t1, t0, Alu.subtract)
                v.tensor_tensor(t3, t3, t2, Alu.mult)
                v.tensor_tensor(t0, t0, t3, Alu.add)
                v.tensor_tensor(t3, t0, t0, Alu.mult)
                v.tensor_scalar(t3, t3, OVT2, BIG, Alu.is_gt, Alu.mult)
                v.scalar_tensor_tensor(dst, t3, 1.0, t0, Alu.add, Alu.mult)
            VB(body)

        def emit_quant(src, dst, t0, t1, t2, mode, gates=(), rows=128,
                       unsigned=False, sub_scalar=False):
            if mode == "e4m3":
                cve, SSC, SISC, THRSQ, OVT2 = q_cve4, 512.0, 2.0**-9, 2.0**-12, 61000.0
            else:
                cve, SSC, SISC, THRSQ, OVT2 = q_cvef, 2.0**24, 2.0**-24, 2.0**-28, 4292870144.0
            cve, cb, ncb = cve[0:rows, :], q_cb[0:rows, :], q_ncb[0:rows, :]

            if sub_scalar:
                def sbody(s_):
                    for sem, cnt in gates:
                        s_.wait_ge(sem.h, cnt)
                    s_.activation(t1, src, Act.Identity, bias=cb, scale=SSC)
                SCB(sbody)
                sgates = [(s_s, s_s.count)]

            def body(v):
                for sem, cnt in gates:
                    v.wait_ge(sem.h, cnt)
                if sub_scalar:
                    for sem, cnt in sgates:
                        v.wait_ge(sem.h, cnt)
                v.tensor_scalar(t0, src, cve, None, Alu.mult)
                v.tensor_tensor(t1 if not sub_scalar else t2, t0, src, Alu.subtract)
                v.tensor_tensor(t0, t0, t1 if not sub_scalar else t2, Alu.subtract)
                if not sub_scalar:
                    v.tensor_scalar(t1, src, SSC, cb, Alu.mult, Alu.add)
                v.tensor_scalar(t1, t1, ncb, SISC, Alu.add, Alu.mult)
                if unsigned:
                    v.tensor_scalar(t2, src, float(2.0 ** round(math.log2(THRSQ) / 2)),
                                    None, Alu.is_lt)
                else:
                    v.tensor_tensor(t2, src, src, Alu.mult)
                    v.tensor_scalar(t2, t2, THRSQ, None, Alu.is_lt)
                v.tensor_tensor(t1, t1, t0, Alu.subtract)
                v.tensor_tensor(t1, t1, t2, Alu.mult)
                v.tensor_tensor(t0, t0, t1, Alu.add)
                if unsigned:
                    thr = 248.0 if mode == "e4m3" else 65520.0
                    v.tensor_scalar(t2, t0, thr, BIG, Alu.is_gt, Alu.mult)
                else:
                    v.tensor_tensor(t2, t0, t0, Alu.mult)
                    v.tensor_scalar(t2, t2, OVT2, BIG, Alu.is_gt, Alu.mult)
                v.scalar_tensor_tensor(dst, t2, 1.0, t0, Alu.add, Alu.mult)
            VB(body)

        ln_done = {}
        a_mm_done = {}
        a_tb = {}
        a_tdone = {}
        zfm = {}

        def a_front(t):
            s, ti = divmod(t, NT)
            xb = xin[:, (t % 2) * D:(t % 2 + 1) * D]
            sx = s_xa[t % 2]

            def body(sync, t=t, s=s, ti=ti, xb=xb, sx=sx):
                if t >= 2:
                    sync.wait_ge(s_v.h, ln_done[t - 2])
                else:
                    sync.wait_ge(s_v.h, rowadd_v)
                    sync.wait_ge(s_row0.h, s_row0.count)
                ins = sync.dma_start(out=xb, in_=x_h.ap()[s, 128 * ti:128 * (ti + 1), :])
                sx.dma(ins)
            block.sync(body)
            x_loaded = sx.count

            def body(v, t=t, sx=sx, x_loaded=x_loaded):
                v.wait_ge(sx.h, x_loaded)
                if t == 0:
                    v.wait_ge(s_row0.h, s_row0.count)
                    v.wait_ge(s_bias.h, s_bias.count)
            block.vector(body)
            ln_tile(D, xb, zt_a, xmt_a, zt_a)
            ln_done[t] = s_v.count

        def a_transp(t):
            tb = []
            a_tb[t] = tb

            def body(te, zr=ln_done[t]):
                te.wait_ge(s_v.h, zr)
                for half in range(2):
                    b = get_bank(te)
                    tb.append(b)
                    for cc in range(4):
                        c = 4 * half + cc
                        ins = te.transpose(PB[b][:, 128 * cc:128 * (cc + 1)],
                                           zt_a[:, 128 * c:128 * (c + 1)], ident[:])
                    s_mm.inc(ins)
            block.tensor(body)
            a_tdone[t] = s_mm.count

        def a_evac_qk(t):
            s, ti = divmod(t, NT)
            tb = a_tb[t]
            zp = t % 2

            def body(v, t=t, tb=tb, zp=zp, tdone=a_tdone[t]):
                v.wait_ge(s_mm.h, tdone)
                if t > 1:
                    v.wait_ge(s_mm.h, a_mm_done[t - 2])
                for half in range(2):
                    sl = slice(512 * half, 512 * (half + 1))
                    v.tensor_copy(zbh[zp][:, sl], PB[tb[half]][:])
                    v.tensor_tensor(zbl[zp][:, sl], PB[tb[half]][:], zbh[zp][:, sl],
                                    Alu.subtract)
            VB(body)
            for half in range(2):
                bank_done(tb[half], s_v)
            zfm[t] = zfm_ready = s_v.count

            qb = []
            def body(te, zp=zp, zfm_ready=zfm_ready):
                te.wait_ge(s_v.h, zfm_ready)
                qb.append(get_bank(te))
                for c in range(8):
                    cs = slice(128 * c, 128 * (c + 1))
                    te.matmul(PB[qb[0]][:, 0:128], wqkh[:, cs], zbh[zp][:, cs],
                              start=(c == 0), stop=False)
                    te.matmul(PB[qb[0]][:, 0:128], wqkh[:, cs], zbl[zp][:, cs],
                              start=False, stop=False)
                    ins = te.matmul(PB[qb[0]][:, 0:128], wqkl[:, cs], zbh[zp][:, cs],
                                    start=False, stop=(c == 7))
                s_mm.inc(ins)
            block.tensor(body)
            qk_mm = s_mm.count

            def body(s_, qk_mm=qk_mm, qb=qb):
                s_.wait_ge(s_mm.h, qk_mm)
                s_.activation(qkt[:], PB[qb[0]][:, 0:128], Act.Silu, bias=bqk_eff[:])
            SCB(body)
            qkt_done = s_s.count

            def body(v, t=t, qkt_done=qkt_done):
                v.wait_ge(s_s.h, qkt_done)
                v.wait_ge(s_qts.h, 64 * t)
                v.tensor_scalar(qt[:], qkt[:], g0[:], be0[:], Alu.mult, Alu.add)
                v.tensor_scalar(kt[:], qkt[:], g1[:], be1[:], Alu.mult, Alu.add)
                v.tensor_copy(qth[:], qt[:])
                v.tensor_tensor(qtl[:], qt[:], qth[:], Alu.subtract)
                v.tensor_copy(kth[:], kt[:])
                v.tensor_tensor(ktl[:], kt[:], kth[:], Alu.subtract)
            VB(body)
            bank_done(qb[0], s_v)
            qk_done = s_v.count

            def body(s_, s=s, ti=ti, qk_done=qk_done):
                s_.wait_ge(s_v.h, qk_done)
                cs = slice(128 * ti, 128 * (ti + 1))
                for src, dst in ((qth, sc_qh), (qtl, sc_ql), (kth, sc_kh), (ktl, sc_kl)):
                    ins = s_.dma_start(out=dst.ap()[s, :, cs], in_=src[:])
                    s_qts.dma(ins)
            block.scalar(body)

        def a_cb(t, cb):
            s, ti = divmod(t, NT)
            hb = []
            zp = t % 2

            def body(te, cb=cb, zp=zp, zfm_ready=zfm[t]):
                te.wait_ge(s_v.h, zfm_ready)
                if cb == 0:
                    te.wait_ge(s_bias.h, s_bias.count)
                hb.append(get_bank(te))
                for c in range(8):
                    zs = slice(128 * c, 128 * (c + 1))
                    ws = slice(c * H2 + 512 * cb, c * H2 + 512 * (cb + 1))
                    te.matmul(PB[hb[0]][:], zbh[zp][:, zs], wba[:, ws],
                              start=(c == 0), stop=False)
                    te.matmul(PB[hb[0]][:], zbh[zp][:, zs], wbb[:, ws],
                              start=False, stop=False)
                    ins = te.matmul(PB[hb[0]][:], zbl[zp][:, zs], wba[:, ws],
                                    start=False, stop=(c == 7))
                s_mm.inc(ins)
            block.tensor(body)
            this_mm = s_mm.count
            if cb < 4:
                eb = ev[cb % 2][:]
                seb = None
            else:
                eb = ev[2 + cb % 2][:]
                seb = s_ev[2 + cb % 2]

            def body(v, cb=cb, eb=eb, seb=seb, hb=hb, this_mm=this_mm):
                v.wait_ge(s_mm.h, this_mm)
                if seb is not None:
                    v.wait_ge(seb.h, seb.count)
                v.tensor_tensor(eb, PB[hb[0]][:], biash[:, 512 * cb:512 * (cb + 1)], Alu.add)
            VB(body)
            bank_done(hb[0], s_v)

            def body(s_, eb=eb):
                s_.wait_ge(s_v.h, s_v.count)
                s_.activation(eb, eb, Act.Silu)
            SCB(body)
            silu_done = s_s.count

            if cb < 4:
                def body(v, eb=eb, silu_done=silu_done):
                    v.wait_ge(s_s.h, silu_done)
                    v.wait_ge(s_ev[0].h, s_ev[0].count)
                    v.wait_ge(s_ev[1].h, s_ev[1].count)
                    v.tensor_copy(vsph[:], eb)
                    v.tensor_tensor(vspl[:], eb, vsph[:], Alu.subtract)
                VB(body)
                vsplit_done = s_v.count

                def body(s_, s=s, ti=ti, cb=cb, vsplit_done=vsplit_done):
                    s_.wait_ge(s_v.h, vsplit_done)
                    rs = slice(128 * ti, 128 * (ti + 1))
                    ins = s_.dma_start(out=sc_vh.ap()[s, rs, 512 * cb:512 * (cb + 1)],
                                       in_=vsph[:])
                    s_ev[0].dma(ins)
                    ins = s_.dma_start(out=sc_vl.ap()[s, rs, 512 * cb:512 * (cb + 1)],
                                       in_=vspl[:])
                    s_ev[1].dma(ins)
                block.scalar(body)
            else:
                def body(s_, s=s, ti=ti, cb=cb, eb=eb, seb=seb, silu_done=silu_done):
                    s_.wait_ge(s_s.h, silu_done)
                    col = 512 * (cb - 4)
                    ins = s_.dma_start(out=sc_g.ap()[s, 128 * ti:128 * (ti + 1),
                                                     col:col + 512], in_=eb)
                    seb.dma(ins)
                block.scalar(body)

        a_front(0)
        a_transp(0)
        a_evac_qk(0)
        for t in range(TT):
            for cb in range(8):
                a_cb(t, cb)
                if cb == 0 and t + 1 < TT:
                    a_front(t + 1)
                if cb == 3 and t + 1 < TT:
                    a_transp(t + 1)
                if cb == 5 and t + 1 < TT:
                    a_evac_qk(t + 1)
            a_mm_done[t] = s_mm.count

        barrier()

        w_cursor = [0]
        w_loaded = {}

        def emit_wload(i):
            if i >= 16:
                return
            c, hf = divmod(i, 2)

            def body(g, i=i, c=c, hf=hf):
                if i >= 1:
                    g.wait_ge(s_v.h, ps[f"wq_{i - 1}"])
                else:
                    g.wait_ge(s_v.h, lnbc_ready)
                    g.wait_ge(s_row0.h, s_row0.count)
                ins = g.dma_start(out=w2stB,
                                  in_=w2_h.ap()[128 * c:128 * (c + 1),
                                                2048 * hf:2048 * (hf + 1)])
                s_wcg.dma(ins)
            block.gpsimd(body)
            w_loaded[i] = s_wcg.count

        def emit_wpieces(n):
            for _ in range(n):
                k = w_cursor[0]
                if k >= 64:
                    return
                w_cursor[0] = k + 1
                i, sub = divmod(k, 4)
                sws = s_ws[i % 2]
                gates = ([(s_wcg, w_loaded[i]), (sws, sws.count)]
                         if sub == 0 else ())
                emit_quant(w2stB[:, 512 * sub:512 * (sub + 1)],
                           wqb[i % 2][:, 512 * sub:512 * (sub + 1)],
                           qt_wB[0], qt_wB[1], qt_wB[2], "e4m3", gates=gates)
                if sub == 3:
                    ps[f"wq_{i}"] = s_v.count
                    c, hf = divmod(i, 2)

                    def body(g, i=i, c=c, hf=hf, sws=sws):
                        g.wait_ge(s_v.h, ps[f"wq_{i}"])
                        ins = g.dma_start(
                            out=bass.AP(sc_wq2, 128 * c * HF + 2048 * hf,
                                        [[HF, 128], [1, 2048]]),
                            in_=wqb[i % 2][:])
                        sws.dma(ins)
                    block.gpsimd(body)
                    emit_wload(i + 1)

        tbanks = []
        ps["aT_consumed"] = 0
        emit_wload(0)

        for s in range(SEQ):
            def body(sync, s=s):
                sync.wait_ge(s_mm.h, s_mm.count)
                for src, dst in ((sc_qh, qfh), (sc_ql, qfl), (sc_kh, kfh)):
                    sync.dma_start(out=dst, in_=src.ap()[s, :, :]).then_inc(s_bqk.h, 16)
                ins = sync.dma_start(out=kfl, in_=sc_kl.ap()[s, :, :])
                s_bqk.dma(ins)
            block.sync(body)
            s_bqk.count += 48
            qk_loaded = s_bqk.count

            for g in range(4):
                nk = 4 * g + 4
                for j in range(nk):
                    def body(te, j=j, g=g, qk_loaded=qk_loaded):
                        te.wait_ge(s_bqk.h, qk_loaded)
                        b = get_bank(te)
                        tbanks.append(b)
                        js = slice(128 * j, 128 * (j + 1))
                        gs = slice(512 * g, 512 * (g + 1))
                        te.matmul(PB[b][:], kfh[:, js], qfh[:, gs], start=True, stop=False)
                        te.matmul(PB[b][:], kfh[:, js], qfl[:, gs], start=False, stop=False)
                        ins = te.matmul(PB[b][:], kfl[:, js], qfh[:, gs],
                                        start=False, stop=True)
                        s_mm.inc(ins)
                    block.tensor(body)
                    this_mm = s_mm.count
                    sbank = tbanks[-1]

                    def body(s_, j=j, sbank=sbank, this_mm=this_mm):
                        s_.wait_ge(s_mm.h, this_mm)
                        s_.wait_ge(s_mm.h, ps["aT_consumed"])
                        s_.activation(aT[j], PB[sbank][:], Act.Relu)
                    SCB(body)

                    def body(v, j=j, g=g):
                        v.wait_ge(s_s.h, s_s.count)
                        if j >= 4 * g:
                            off = 384 - (128 * j - 512 * g)
                            v.tensor_tensor(aT[j], aT[j], rmask[:, off:off + 512], Alu.mult)
                        v.tensor_tensor(aT[j], aT[j], aT[j], Alu.mult)
                        v.tensor_copy(aTh[j], aT[j])
                        v.tensor_tensor(aTl[j], aT[j], aTh[j], Alu.subtract)
                    VB(body)
                    bank_done(sbank, s_v)
                    ps[f"aT_split_{j}"] = s_v.count
                tbanks.clear()
                simT_done = s_v.count

                for dq in range(4):
                    vgbh = vgh[dq % 2]
                    vgbl = vgl[dq % 2]
                    svg = s_vg[dq % 2]

                    def body(sync, s=s, dq=dq, nk=nk, vgbh=vgbh, vgbl=vgbl, svg=svg):
                        sync.wait_ge(s_mm.h, ps.get(f"vg_used_{dq % 2}", 0))
                        for j in range(nk):
                            js = slice(512 * j, 512 * (j + 1))
                            src = (s, slice(128 * j, 128 * (j + 1)),
                                   slice(512 * dq, 512 * (dq + 1)))
                            sync.dma_start(out=vgbh[:, js],
                                           in_=sc_vh.ap()[src]).then_inc(svg.h, 16)
                            ins = sync.dma_start(out=vgbl[:, js], in_=sc_vl.ap()[src])
                            svg.dma(ins)
                            svg.count += 16
                    block.sync(body)
                    vg_loaded = svg.count

                    obanks = []
                    splits = [ps[f"aT_split_{j}"] for j in range(nk)]
                    def body(te, g=g, nk=nk, vgbh=vgbh, vgbl=vgbl, svg=svg,
                             vg_loaded=vg_loaded, splits=splits):
                        te.wait_ge(svg.h, vg_loaded)
                        for it in range(4):
                            obanks.append(get_bank(te))
                        for j in range(nk):
                            te.wait_ge(s_v.h, splits[j])
                            js = slice(512 * j, 512 * (j + 1))
                            for it in range(max(0, j - 4 * g), 4):
                                its = slice(128 * it, 128 * (it + 1))
                                te.matmul(PB[obanks[it]][:], aTh[j][:, its], vgbh[:, js],
                                          start=(j == 0), stop=False)
                                te.matmul(PB[obanks[it]][:], aTh[j][:, its], vgbl[:, js],
                                          start=False, stop=False)
                                ins = te.matmul(PB[obanks[it]][:], aTl[j][:, its],
                                                vgbh[:, js], start=False,
                                                stop=(j == 4 * g + it))
                        s_mm.inc(ins)
                    block.tensor(body)
                    ps[f"vg_used_{dq % 2}"] = s_mm.count
                    av_mm = s_mm.count

                    emit_wpieces({1: 2, 2: 3, 3: 3}.get(g, 0))

                    for it in range(4):
                        seb = s_ev[it]

                        def body(v, it=it, g=g, seb=seb, av_mm=av_mm):
                            v.wait_ge(s_mm.h, av_mm)
                            v.wait_ge(seb.h, seb.count)
                            v.tensor_scalar(ev[it][:], PB[obanks[it]][:],
                                            rsc2[:, 4 * g + it:4 * g + it + 1], None, Alu.mult)
                        VB(body)
                        bank_done(obanks[it], s_v)

                        def body(s_, s=s, g=g, it=it, dq=dq, seb=seb):
                            s_.wait_ge(s_v.h, s_v.count)
                            row = 128 * (4 * g + it)
                            ins = s_.dma_start(out=sc_ua.ap()[s, row:row + 128, 512 * dq:512 * (dq + 1)],
                                               in_=ev[it][:])
                            seb.dma(ins)
                        block.scalar(body)
                ps["aT_consumed"] = s_mm.count
        emit_wpieces(64)

        barrier()

        for c in range(16):
            swo = s_c[c % 2]

            def body(sync, c=c, swo=swo):
                if c >= 2:
                    sync.wait_ge(s_v.h, ps[f"wo_split_{c - 2}"])
                ins = sync.dma_start(out=wostg[c % 2],
                                     in_=wo_h.ap()[128 * c:128 * (c + 1), :])
                swo.dma(ins)
                if c == 0:
                    ins = sync.dma_start(out=bqt[:], in_=bass.AP(bo_h, 0, [[0, 128], [1, D]]))
                    s_wo.dma(ins)
            block.sync(body)
            wo_c = swo.count

            def body(v, c=c, swo=swo, wo_c=wo_c):
                v.wait_ge(swo.h, wo_c)
                if c == 0:
                    v.wait_ge(s_wo.h, s_wo.count)
                dst = slice(1024 * c, 1024 * (c + 1))
                v.tensor_copy(woh[:, dst], wostg[c % 2])
                v.tensor_tensor(wol[:, dst], wostg[c % 2], woh[:, dst], Alu.subtract)
            VB(body)
            ps[f"wo_split_{c}"] = s_v.count
        wo_ready = s_v.count

        c_used = {}

        def emit_c_ln(tp):
            dbp = tp % 2
            sp, tip = divmod(tp, NT)
            sxpp = s_x2sp if dbp == 0 else s_x2b
            src = x_db[dbp]

            def body(v, src=src):
                v.wait_ge(s_x2sp.h, s_x2sp.count)
                v.wait_ge(s_x2b.h, s_x2b.count)
                v.reduce_sum(st1[:], src, axis=AX.X)
                v.tensor_scalar(st2[:], st1[:], 1.0 / D, None, Alu.mult)
                for hh in range(2):
                    hs = slice(512 * hh, 512 * (hh + 1))
                    v.tensor_scalar(ev[hh][:], src[:, hs], st2[:], None, Alu.subtract)
                    v.scalar_tensor_tensor(ev[2 + hh][:], ev[hh][:], 1.0, ev[hh][:],
                                           Alu.mult, Alu.mult,
                                           accum_out=(st3 if hh == 0 else st4)[:])
                v.tensor_tensor(st3[:], st3[:], st4[:], Alu.add)
            VB(body)
            c_used[tp] = s_v.count

            def body(s_):
                s_.wait_ge(s_v.h, s_v.count)
                s_.activation(st4[:], st3[:], Act.Ln, bias=eps_t[:], scale=1.0 / D)
                s_.activation(st4[:], st4[:], Act.Exp, scale=-0.5)
            SCB(body)

            def body(v):
                v.wait_ge(s_s.h, s_s.count)
                for hh in range(2):
                    v.tensor_scalar(ev[hh][:], ev[hh][:], st4[:], None, Alu.mult)
            VB(body)

            def body(s_, sp=sp, tip=tip, sxpp=sxpp):
                s_.wait_ge(s_v.h, s_v.count)
                for hh in range(2):
                    ins = s_.dma_start(
                        out=sc_x2.ap()[sp, 128 * tip:128 * (tip + 1),
                                       512 * hh:512 * (hh + 1)],
                        in_=ev[hh][:])
                    sxpp.dma(ins)
            block.scalar(body)

        c_loaded = {}
        c_xloaded = {}

        def c_loads(t):
            s, ti = divmod(t, NT)
            db = t % 2
            sc = s_c[db]

            def body(sync, s=s, ti=ti, db=db, t=t, sc=sc):
                if t >= 2:
                    sync.wait_ge(s_v.h, ps[f"c_og_{t - 2}"])
                else:
                    sync.wait_ge(s_v.h, wo_ready)
                sync.dma_start(out=ua_db[db], in_=sc_ua.ap()[s, 128 * ti:128 * (ti + 1), :]).then_inc(sc.h, 16)
                ins = sync.dma_start(out=gt_db[db], in_=sc_g.ap()[s, 128 * ti:128 * (ti + 1), :])
                sc.dma(ins)
            block.sync(body)
            sc.count += 16
            c_loaded[t] = sc.count

        def c_xload(t):
            s, ti = divmod(t, NT)
            db = t % 2
            sxp = s_x2sp if db == 0 else s_x2b
            sxl = s_xd1 if db == 0 else s_xd1b

            def body(sync, s=s, ti=ti, db=db, t=t, sxp=sxp, sxl=sxl):
                if t >= 2:
                    sync.wait_ge(s_v.h, c_used[t - 2])
                    sync.wait_ge(sxp.h, 32 * (t // 2))
                ins = sync.dma_start(out=x_db[db], in_=x_h.ap()[s, 128 * ti:128 * (ti + 1), :])
                sxl.dma(ins)
            block.sync(body)
            c_xloaded[t] = (sxl, sxl.count)

        def c_og(t):
            db = t % 2
            sc = s_c[db]

            def body(v, db=db, sc=sc, cl=c_loaded[t],
                     ogfree=ps.get("c_tdone", 0)):
                v.wait_ge(sc.h, cl)
                v.wait_ge(s_mm.h, ogfree)
                v.tensor_tensor(og_t, ua_db[db], gt_db[db], Alu.mult)
            VB(body)
            ps[f"c_og_{t}"] = s_v.count

        c_tb = {}

        def c_transp(t):
            tb = []
            c_tb[t] = tb

            def body(te, og_ready=ps[f"c_og_{t}"]):
                te.wait_ge(s_v.h, og_ready)
                for qtr in range(4):
                    b = get_bank(te)
                    tb.append(b)
                    for cc in range(4):
                        c = 4 * qtr + cc
                        ins = te.transpose(PB[b][:, 128 * cc:128 * (cc + 1)],
                                           og_t[:, 128 * c:128 * (c + 1)], ident[:])
                    s_mm.inc(ins)
            block.tensor(body)
            ps["c_tdone"] = ps[f"c_td_{t}"] = s_mm.count

        def c_evac(t):
            tb = c_tb[t]
            db = t % 2

            def body(v, tb=tb, db=db, tdone=ps[f"c_td_{t}"]):
                v.wait_ge(s_mm.h, tdone)
                for qtr in range(4):
                    sl = slice(512 * qtr, 512 * (qtr + 1))
                    v.tensor_copy(oghm[db][:, sl], PB[tb[qtr]][:])
                    v.tensor_tensor(oglm[db][:, sl], PB[tb[qtr]][:], oghm[db][:, sl],
                                    Alu.subtract)
            VB(body)
            for qtr in range(4):
                bank_done(tb[qtr], s_v)
            ps[f"c_ogfm_{t}"] = s_v.count

        c_ob = {}

        def c_mm_half(t, b2):
            db = t % 2
            if b2 == 0:
                c_ob[t] = []
            ob2 = c_ob[t]

            def body(te, b2=b2, db=db, ogfm_ready=ps[f"c_ogfm_{t}"]):
                te.wait_ge(s_v.h, ogfm_ready)
                b = get_bank(te)
                ob2.append(b)
                for c in range(16):
                    ogs = slice(128 * c, 128 * (c + 1))
                    ws = slice(1024 * c + 512 * b2, 1024 * c + 512 * (b2 + 1))
                    te.matmul(PB[b][:], oghm[db][:, ogs], woh[:, ws],
                              start=(c == 0), stop=False)
                    te.matmul(PB[b][:], oghm[db][:, ogs], wol[:, ws],
                              start=False, stop=False)
                    ins = te.matmul(PB[b][:], oglm[db][:, ogs], woh[:, ws],
                                    start=False, stop=(c == 15))
                s_mm.inc(ins)
            block.tensor(body)
            ps[f"c_mm_{t}"] = s_mm.count

        def c_u(t):
            db = t % 2
            ob2 = c_ob[t]

            def body(v, db=db, mm2=ps[f"c_mm_{t}"], ob2=ob2, xl=c_xloaded[t]):
                v.wait_ge(s_mm.h, mm2)
                v.wait_ge(xl[0].h, xl[1])
                for b2 in range(2):
                    v.scalar_tensor_tensor(x_db[db][:, 512 * b2:512 * (b2 + 1)],
                                           x_db[db][:, 512 * b2:512 * (b2 + 1)], 2.0,
                                           PB[ob2[b2]][:], Alu.mult, Alu.add)
                    v.tensor_tensor(x_db[db][:, 512 * b2:512 * (b2 + 1)],
                                    x_db[db][:, 512 * b2:512 * (b2 + 1)],
                                    bqt[:, 512 * b2:512 * (b2 + 1)], Alu.add)
            VB(body)
            for b2 in range(2):
                bank_done(ob2[b2], s_v)

        c_loads(0)
        c_xload(0)
        c_og(0)
        c_transp(0)
        c_evac(0)
        for t in range(TT):
            if t + 1 < TT:
                c_loads(t + 1)
            if t > 0:
                emit_c_ln(t - 1)
            if t + 1 < TT:
                c_og(t + 1)
                c_xload(t + 1)
            c_mm_half(t, 0)
            if t + 1 < TT:
                c_transp(t + 1)
            c_mm_half(t, 1)
            if t + 1 < TT:
                c_evac(t + 1)
            c_u(t)
        emit_c_ln(TT - 1)

        barrier()

        wr = lambda i: wmid[0:1, D * i:D * (i + 1)]
        def body(sync):
            sync.wait_ge(s_v.h, s_v.count)
            ins = sync.dma_start(out=wr(0), in_=bass.AP(b2_h, 0, [[D, 1], [1, D]]))
            s_wb2.dma(ins)
        block.sync(body)
        b2_loaded = s_wb2.count
        emit_quant(wr(0), wr(1), wr(2), wr(3), wr(4), "fp16",
                   gates=[(s_wb2, b2_loaded), (s_mm, s_mm.count)], rows=1)

        def body(s_):
            s_.wait_ge(s_v.h, s_v.count)
            ins = s_.dma_start(out=sc_row.ap()[1:2, 0:D], in_=wr(1))
            s_row1.dma(ins)
        block.scalar(body)

        barrier()

        w1h = wba[:, 0:16384]
        w1l = wbb[:, 0:16384]
        qd = {0: 0, 1: 0}
        qg = {0: 0, 1: 0}
        for half in range(2):
            for hcl in range(16):
                hc = 16 * half + hcl
                sw = s_w1[hcl % 2]

                def body(sync, hc=hc, hcl=hcl, half=half, sw=sw):
                    if hcl >= 2:
                        sync.wait_ge(s_mm.h, ps[f"w1t_{hcl - 2}"])
                    elif half == 1:
                        sync.wait_ge(s_mm.h, ps["ffn1_mm"])
                    ins = sync.dma_start(out=w1st[hcl % 2],
                                         in_=w1_h.ap()[128 * hc:128 * (hc + 1), :])
                    sw.dma(ins)
                block.sync(body)
                w1_loaded = sw.count

                tb4 = []
                def body(te, hcl=hcl, sw=sw, w1_loaded=w1_loaded):
                    te.wait_ge(sw.h, w1_loaded)
                    for hh in range(2):
                        b = get_bank(te)
                        tb4.append(b)
                        for cc in range(4):
                            c = 4 * hh + cc
                            ins = te.transpose(PB[b][:, 128 * cc:128 * (cc + 1)],
                                               w1st[hcl % 2][:, 128 * c:128 * (c + 1)], ident[:])
                        s_mm.inc(ins)
                block.tensor(body)
                ps[f"w1t_{hcl}"] = s_mm.count
                tdone = s_mm.count

                def body(v, hcl=hcl, tdone=tdone):
                    v.wait_ge(s_mm.h, tdone)
                    for hh in range(2):
                        off = 8192 * hh + 128 * hcl
                        dh = bass.AP(wba, off, [[8 * H2, 128], [2048, 4], [1, 128]])
                        dl = bass.AP(wbb, off, [[8 * H2, 128], [2048, 4], [1, 128]])
                        v.tensor_copy(dh, PB[tb4[hh]][:])
                        v.tensor_tensor(dl, PB[tb4[hh]][:], dh, Alu.subtract)
                VB(body)
                for hh in range(2):
                    bank_done(tb4[hh], s_v)
            w1t_ready_v = s_v.count

            def d1_x2pipe(grp, half=half):
                sg, gl = divmod(grp, 4)
                gp = grp % 2
                for itl in range(4):
                    ti = 4 * gl + itl
                    sxl = s_xd1 if itl % 2 == 0 else s_xd1b

                    def body(sync, sg=sg, ti=ti, itl=itl, grp=grp, half=half,
                             gp=gp, sxl=sxl):
                        if grp == 0 and half == 0 and itl == 0:
                            sync.wait_ge(s_v.h, w1t_ready_v)
                        if itl == 0:
                            sync.wait_ge(s_mm.h, ps.get(f"ffn1_{gp}", 0))
                            if half > 0 and grp == 0:
                                sync.wait_ge(s_mm.h, ps["ffn1_mm"])
                        sync.wait_ge(s_mm.h, ps.get(f"d1_xt_{itl % 2}", 0))
                        ins = sync.dma_start(out=xmt_d1[itl % 2],
                                             in_=sc_x2.ap()[sg, 128 * ti:128 * (ti + 1), :])
                        sxl.dma(ins)
                    block.sync(body)
                    x2_loaded = sxl.count

                    tb4 = []
                    def body(te, itl=itl, sxl=sxl, x2_loaded=x2_loaded):
                        te.wait_ge(sxl.h, x2_loaded)
                        for hh in range(2):
                            b = get_bank(te)
                            tb4.append(b)
                            for cc in range(4):
                                c = 4 * hh + cc
                                ins = te.transpose(PB[b][:, 128 * cc:128 * (cc + 1)],
                                                   xmt_d1[itl % 2][:, 128 * c:128 * (c + 1)],
                                                   ident[:])
                            s_mm.inc(ins)
                    block.tensor(body)
                    ps[f"d1_xt_{itl % 2}"] = s_mm.count
                    tdone = s_mm.count

                    def body(v, itl=itl, gp=gp, tdone=tdone):
                        v.wait_ge(s_mm.h, tdone)
                        if itl == 0:
                            v.wait_ge(s_xb[gp].h, ps.get(f"xbs_{gp}", 0))
                        for hh in range(2):
                            off = 16384 + 4096 * gp + 2048 * hh + 128 * itl
                            dh = bass.AP(wba, off, [[8 * H2, 128], [512, 4], [1, 128]])
                            dl = bass.AP(wbb, off, [[8 * H2, 128], [512, 4], [1, 128]])
                            v.tensor_copy(dh, PB[tb4[hh]][:])
                            v.tensor_tensor(dl, PB[tb4[hh]][:], dh, Alu.subtract)
                    VB(body)
                    for hh in range(2):
                        bank_done(tb4[hh], s_v)
                if half == 0:
                    def body(g, grp=grp, gp=gp, evdone=s_v.count):
                        g.wait_ge(s_v.h, evdone)
                        for hl, arena in ((0, x2h[gp]), (1, x2l[gp])):
                            ins = g.dma_start(out=sc_xb.ap()[grp, hl, :, :],
                                              in_=arena)
                            s_xb[gp].dma(ins)
                    block.gpsimd(body)
                    ps[f"xbs_{gp}"] = s_xb[gp].count
                return (s_v, s_v.count)

            def d1_x2load(grp):
                gp = grp % 2
                sxl = s_xd1 if gp == 0 else s_xd1b

                def body(sync, grp=grp, gp=gp, sxl=sxl):
                    sync.wait_ge(s_mm.h, ps.get(f"ffn1_{gp}", 0))
                    if grp <= 1:
                        sync.wait_ge(s_mm.h, ps["ffn1_mm"])
                        sync.wait_ge(s_xb[0].h, s_xb[0].count)
                        sync.wait_ge(s_xb[1].h, s_xb[1].count)
                    for hl, arena in ((0, x2h[gp]), (1, x2l[gp])):
                        ins = sync.dma_start(out=arena, in_=sc_xb.ap()[grp, hl, :, :])
                        sxl.dma(ins)
                block.sync(body)
                return (sxl, sxl.count)

            if half == 0:
                x2r = {0: d1_x2pipe(0)}
            else:
                x2r = {0: d1_x2load(0)}
            for grp in range(8):
                sg, gl = divmod(grp, 4)
                gp = grp % 2
                x2fm_ready = x2r[grp]

                for hcp in range(8):
                    e = hcp % 2

                    for sub in range(2):
                        hc = 16 * half + 2 * hcp + sub
                        hcl = 2 * hcp + sub

                        def body(te, hcl=hcl, gp=gp, x2fm_ready=x2fm_ready):
                            te.wait_ge(x2fm_ready[0].h, x2fm_ready[1])
                            b = get_bank(te)
                            tbanks.append(b)
                            for c in range(8):
                                ws = slice(c * 2048 + 128 * hcl, c * 2048 + 128 * (hcl + 1))
                                xs = slice(512 * c, 512 * (c + 1))
                                te.matmul(PB[b][:], w1h[:, ws], x2h[gp][:, xs],
                                          start=(c == 0), stop=False)
                                te.matmul(PB[b][:], w1h[:, ws], x2l[gp][:, xs],
                                          start=False, stop=False)
                                ins = te.matmul(PB[b][:], w1l[:, ws], x2h[gp][:, xs],
                                                start=False, stop=(c == 7))
                            s_mm.inc(ins)
                        block.tensor(body)
                        fbank = tbanks[-1]
                        this_mm = s_mm.count

                        def body(s_, hc=hc, sub=sub, grp=grp, half=half, e=e,
                                 fbank=fbank, this_mm=this_mm, evfree=qd[e]):
                            s_.wait_ge(s_mm.h, this_mm)
                            if grp == 0 and half == 0 and hc < 2:
                                s_.wait_ge(s_v.h, lnbc_ready)
                            elif sub == 0:
                                s_.wait_ge(s_v.h, evfree)
                            s_.activation(qsrc[e][:, 512 * sub:512 * (sub + 1)],
                                          PB[fbank][:], Act.Relu, bias=b1tab[:, hc:hc + 1])
                        SCB(body)
                        bank_done(fbank, s_s)

                    emit_quant(qsrc[e][:], hqb[e][:], qd_t0, qd_t12[e][0],
                               qd_t12[e][1], "e4m3", unsigned=True,
                               gates=[(s_s, s_s.count), (s_hq[e], s_hq[e].count)])
                    qd[e] = s_v.count

                    def body(g, gl=gl, sg=sg, half=half, hcp=hcp, e=e, qdone=qd[e]):
                        g.wait_ge(s_v.h, qdone)
                        tt0 = 16 * sg + 4 * gl
                        for sub in range(2):
                            hc = 16 * half + 2 * hcp + sub
                            ins = g.dma_start(
                                out=bass.AP(sc_hq, tt0 * 32 * 16384 + hc * 16384,
                                            [[128, 128], [32 * 16384, 4], [1, 128]]),
                                in_=hqb[e][:, 512 * sub:512 * (sub + 1)].rearrange(
                                    "p (a b) -> p a b", b=128))
                            s_hq[e].dma(ins)
                    block.gpsimd(body)
                    if hcp == 3 and grp + 1 < 8:
                        x2r[grp + 1] = (d1_x2pipe(grp + 1) if half == 0
                                        else d1_x2load(grp + 1))
                ps["ffn1_mm"] = ps[f"ffn1_{gp}"] = s_mm.count
                tbanks.clear()

        barrier()

        def body(sync):
            for hc in range(32):
                ins = sync.dma_start_transpose(
                    out=wba[:, 1024 * hc:1024 * (hc + 1)],
                    in_=bass.AP(sc_wq2, 128 * hc, [[HF, D], [1, 128]]))
                s_wq.dma(ins)
            ins = sync.dma_start(out=bqt[:], in_=bass.AP(sc_row, H2, [[0, 128], [1, D]]))
            s_wq.dma(ins)
        block.sync(body)
        wq_loaded = s_wq.count

        s_d2db = [s_d2, s_vb[0]]
        s_outdb = [s_out, s_vb[1]]
        d2_pend = []

        def d2_ln_tail():
            tp, sp, tip, dbp, obufp, sdop, scnt = d2_pend.pop(0)

            def body(v, obufp=obufp, scnt=scnt):
                v.wait_ge(s_s.h, scnt)
                v.tensor_scalar(obufp, obufp, st4[:], None, Alu.mult)
            VB(body)
            ps[f"d2_out_{dbp}"] = s_v.count

            def body(s_, sp=sp, tip=tip, obufp=obufp, sdop=sdop):
                s_.wait_ge(s_v.h, s_v.count)
                ins = s_.dma_start(out=out_h.ap()[sp, 128 * tip:128 * (tip + 1), :],
                                   in_=obufp)
                sdop.dma(ins)
            block.scalar(body)

        for t in range(TT):
            s, ti = divmod(t, NT)
            db = t % 2
            sd = s_d2db[db]
            sdo = s_outdb[db]
            hql = hqld[db]
            if t >= 1:
                d2_ln_tail()
            obuf = obuf_db[db]
            oqbuf = oqbuf_db[db]
            u3buf = u3_db[db]

            def body(sync, t=t, s=s, ti=ti, db=db, sd=sd, hql=hql, u3buf=u3buf):
                if t == 0:
                    sync.wait_ge(s_wq.h, wq_loaded)
                if t >= 2:
                    sync.wait_ge(s_mm.h, ps[f"d2_mm_{db}"])
                    sync.wait_ge(s_v.h, ps.get(f"d2_out_{db}", 0))
                sync.dma_start(out=hql.rearrange("p (a b) -> p a b", b=128),
                               in_=bass.AP(sc_hq, t * 32 * 128 * 128,
                                           [[128, 128], [128 * 128, 32], [1, 128]])).then_inc(sd.h, 16)
                ins = sync.dma_start(out=u3buf, in_=sc_x2.ap()[s, 128 * ti:128 * (ti + 1), :])
                sd.dma(ins)
            block.sync(body)
            sd.count += 16
            d2_loaded = sd.count

            ob2 = []
            def body(te, sd=sd, hql=hql, d2_loaded=d2_loaded):
                te.wait_ge(sd.h, d2_loaded)
                for b2 in range(2):
                    b = get_bank(te)
                    ob2.append(b)
                    for hc in range(32):
                        ins = te.matmul(PB[b][:], hql[:, 128 * hc:128 * (hc + 1)],
                                        wba[:, 1024 * hc + 512 * b2:1024 * hc + 512 * (b2 + 1)],
                                        start=(hc == 0), stop=(hc == 31))
                    s_mm.inc(ins)
            block.tensor(body)
            ps[f"d2_mm_{db}"] = s_mm.count
            mm2 = s_mm.count

            def body(v, mm2=mm2, t=t, obuf=obuf, sdo=sdo):
                v.wait_ge(s_mm.h, mm2)
                if t >= 2:
                    v.wait_ge(sdo.h, 16 * (t // 2))
                for b2 in range(2):
                    v.tensor_tensor(obuf[:, 512 * b2:512 * (b2 + 1)], PB[ob2[b2]][:],
                                    bqt[:, 512 * b2:512 * (b2 + 1)], Alu.add)
            VB(body)
            for b2 in range(2):
                bank_done(ob2[b2], s_v)
            ob2.clear()

            emit_quant(obuf, oqbuf, q2_t0, q2_t12[db][0], q2_t12[db][1], "fp16")

            def body(v, sd=sd, u3buf=u3buf, oqbuf=oqbuf, d2_loaded=d2_loaded,
                     pscnt=ps.get("d2_scnt", 0)):
                v.wait_ge(sd.h, d2_loaded)
                v.wait_ge(s_s.h, pscnt)
                v.tensor_tensor(u3buf, u3buf, oqbuf, Alu.add)
                v.reduce_sum(st1[:], u3buf, axis=AX.X)
                v.tensor_scalar(st2[:], st1[:], 1.0 / D, None, Alu.mult)
                v.tensor_scalar(obuf, u3buf, st2[:], None, Alu.subtract)
                v.scalar_tensor_tensor(lnsq_d2, obuf, 1.0, obuf, Alu.mult, Alu.mult,
                                       accum_out=st3[:])
            VB(body)
            ps[f"d2_x2_{db}"] = s_v.count

            def body(s_):
                s_.wait_ge(s_v.h, s_v.count)
                s_.activation(st4[:], st3[:], Act.Ln, bias=eps_t[:], scale=1.0 / D)
                s_.activation(st4[:], st4[:], Act.Exp, scale=-0.5)
            SCB(body)
            ps["d2_scnt"] = s_s.count
            d2_pend.append((t, s, ti, db, obuf, sdo, s_s.count))
        while d2_pend:
            d2_ln_tail()

        def body(sync):
            for sm in ALL_SEMS:
                if sm.count > 0:
                    sync.wait_ge(sm.h, sm.count)
        block.sync(body)

    nc._keepalive = es
    return nc


_NC_CACHE = {}


def kernel(**inputs):
    if "prog" not in _NC_CACHE:
        _NC_CACHE["prog"] = build_program()
    nc = _NC_CACHE["prog"]

    x = np.ascontiguousarray(inputs["L0_attn_x"], dtype=np.float32)
    weights = {k: np.ascontiguousarray(v, dtype=np.float32) for k, v in inputs.items()
               if k != "L0_attn_x"}
    in_maps = []
    for c in range(NCORES):
        m = {"x": np.ascontiguousarray(x[SEQ * c:SEQ * (c + 1)])}
        m.update(weights)
        in_maps.append(m)
    res = run_bass_kernel_spmd(nc, in_maps, core_ids=list(range(NCORES)))
    out = np.concatenate([np.asarray(r["out"]) for r in res.results], axis=0)
    return out.astype(np.float32)


# version 32
# speedup vs baseline: 1.0444x; 1.0444x over previous
import contextlib
import math

import numpy as np

import concourse.bass as bass
import concourse.mybir as mybir
from concourse.bass_utils import run_bass_kernel_spmd

F32 = mybir.dt.float32
BF16 = mybir.dt.bfloat16
Alu = mybir.AluOpType
Act = mybir.ActivationFunctionType
AX = mybir.AxisListType

NCORES = 8
SEQ = 2
N = 2048
D = 1024
H2 = 4096
DV = 2048
QK = 128
HF = 4096
NT = N // 128
TT = SEQ * NT

C32 = float(1.5 * 2.0**23)
C52 = float(1.5 * 2.0**52)
LN2_HI = float(np.float32(math.log(2.0)))
INV_LN2 = float(np.float32(1.0 / math.log(2.0)))
BETA = float(np.float32(math.log(2.0) / LN2_HI - 1.0))
BIG = 3.0e38

TAPS = False


class Sem:
    def __init__(self, es, nc, name):
        self.h = es.enter_context(nc.semaphore(name))
        self.count = 0

    def inc(self, instr, n=1):
        instr.then_inc(self.h, n)
        self.count += n

    def dma(self, instr):
        instr.then_inc(self.h, 16)
        self.count += 16


_WRAP_OPS = {
    "tensor_scalar", "tensor_tensor", "tensor_copy", "tensor_add", "tensor_sub",
    "tensor_mul", "tensor_max", "reduce_sum", "reduce_max", "tensor_reduce",
    "scalar_tensor_tensor", "reciprocal", "memset", "memzero", "iota",
    "affine_select", "engine_nop", "bn_stats", "bn_aggr", "select", "tensor_relu",
    "activation", "copy", "mul", "add", "square", "sign", "sqrt",
}


class SE:

    def __init__(self, eng, sem):
        self._e = eng
        self._s = sem

    def __getattr__(self, n):
        f = getattr(self._e, n)
        if n in _WRAP_OPS:
            def wrapped(*a, **k):
                self._e.wait_ge(self._s.h, self._s.count)
                ins = f(*a, **k)
                self._s.inc(ins)
                return ins
            return wrapped
        if n == "dma_start":
            def wrapped(*a, **k):
                self._e.wait_ge(self._s.h, self._s.count)
                return f(*a, **k)
            return wrapped
        return f


def build_program():
    nc = bass.Bass("TRN2", target_bir_lowering=False)
    es = contextlib.ExitStack()

    x_h = nc.dram_tensor("x", [SEQ, N, D], F32, kind="ExternalInput")
    out_h = nc.dram_tensor("out", [SEQ, N, D], F32, kind="ExternalOutput")
    wh_h = nc.dram_tensor("L0_attn_W_hidden", [D, H2], F32, kind="ExternalInput")
    bh_h = nc.dram_tensor("L0_attn_b_hidden", [H2], F32, kind="ExternalInput")
    wqk_h = nc.dram_tensor("L0_attn_W_qk", [D, QK], F32, kind="ExternalInput")
    bqk_h = nc.dram_tensor("L0_attn_b_qk", [QK], F32, kind="ExternalInput")
    osg_h = nc.dram_tensor("L0_attn_os_gamma", [2, QK], F32, kind="ExternalInput")
    osb_h = nc.dram_tensor("L0_attn_os_beta", [2, QK], F32, kind="ExternalInput")
    wo_h = nc.dram_tensor("L0_attn_W_out", [DV, D], F32, kind="ExternalInput")
    bo_h = nc.dram_tensor("L0_attn_b_out", [D], F32, kind="ExternalInput")
    lng_h = nc.dram_tensor("L0_attn_ln_g", [D], F32, kind="ExternalInput")
    lnb_h = nc.dram_tensor("L0_attn_ln_b", [D], F32, kind="ExternalInput")
    w1_h = nc.dram_tensor("L0_ffn_w1", [HF, D], F32, kind="ExternalInput")
    b1_h = nc.dram_tensor("L0_ffn_b1", [HF], F32, kind="ExternalInput")
    w2_h = nc.dram_tensor("L0_ffn_w2", [D, HF], F32, kind="ExternalInput")
    b2_h = nc.dram_tensor("L0_ffn_b2", [D], F32, kind="ExternalInput")

    sk = "ExternalOutput" if TAPS else "Internal"
    sc_vh = nc.dram_tensor("sc_vh", [SEQ, N, DV], BF16, kind=sk)
    sc_vl = nc.dram_tensor("sc_vl", [SEQ, N, DV], BF16, kind=sk)
    sc_g = nc.dram_tensor("sc_g", [SEQ, N, DV], F32, kind=sk)
    sc_qh = nc.dram_tensor("sc_qh", [SEQ, QK, N], BF16, kind=sk)
    sc_ql = nc.dram_tensor("sc_ql", [SEQ, QK, N], BF16, kind=sk)
    sc_kh = nc.dram_tensor("sc_kh", [SEQ, QK, N], BF16, kind=sk)
    sc_kl = nc.dram_tensor("sc_kl", [SEQ, QK, N], BF16, kind=sk)
    sc_ua = nc.dram_tensor("sc_ua", [SEQ, N, DV], F32, kind=sk)
    sc_x2 = nc.dram_tensor("sc_x2", [SEQ, N, D], F32, kind=sk)
    sc_hq = nc.dram_tensor("sc_hq", [TT, HF // 128, 128, 128], BF16, kind=sk)
    sc_wq2 = nc.dram_tensor("sc_wq2", [D, HF], BF16, kind=sk)
    sc_row = nc.dram_tensor("sc_row", [2, H2], F32, kind="Internal")
    sc_xb = nc.dram_tensor("sc_xb", [8, 2, 128, 4096], BF16, kind="Internal")

    def S(name):
        return Sem(es, nc, name)
    s_mm, s_v, s_s, s_g = S("s_mm"), S("s_v"), S("s_s"), S("s_g")
    s_in = S("s_in")
    s_bias = S("s_bias")
    s_row0, s_row1 = S("s_row0"), S("s_row1")
    s_xa = [S("s_xa0"), S("s_xa1")]
    s_qts = S("s_qts")
    s_ev = [S(f"s_ev{i}") for i in range(4)]
    s_bqk = S("s_bqk")
    s_vb = [S("s_vb0"), S("s_vb1")]
    s_vg = [S("s_vg0"), S("s_vg1")]
    s_c = [S("s_c0"), S("s_c1")]
    s_wo = S("s_wo")
    s_x2sp = S("s_x2sp")
    s_wc = S("s_wc")
    s_wb2 = S("s_wb2")
    s_w1 = [S("s_w1a"), S("s_w1b")]
    s_xd1 = S("s_xd1")
    s_xd1b = S("s_xd1b")
    s_wq = S("s_wq")
    s_d2 = S("s_d2")
    s_out = S("s_out")
    s_hq = [S("s_hq0"), S("s_hq1")]
    s_x2b = S("s_x2b")
    s_wcg = S("s_wcg")
    s_ws = [S("s_ws0"), S("s_ws1")]
    s_xb = [S("s_xba"), S("s_xbb")]
    ALL_SEMS = [s_mm, s_v, s_s, s_g, s_in, s_bias, s_row0, s_row1, *s_xa, s_qts,
                *s_ev, s_bqk, *s_vb, *s_vg, *s_c, s_wo, s_x2sp, s_wc, s_wb2, *s_w1,
                s_xd1, s_xd1b, s_wq, s_d2, s_out, *s_hq, s_x2b,
                s_wcg, *s_ws, *s_xb]

    sb = lambda name, shape, dt=F32: es.enter_context(nc.sbuf_tensor(name, shape, dt))
    ident = sb("ident", [128, 128])
    eps_t = sb("eps_t", [128, 1])
    rsc2 = sb("rsc2", [128, NT])
    rmask = sb("rmask", [128, 896], BF16)
    g0 = sb("g0", [128, 1]); be0 = sb("be0", [128, 1])
    g1 = sb("g1", [128, 1]); be1 = sb("be1", [128, 1])
    bqk_eff = sb("bqk_eff", [128, 1])
    lnb_c = sb("lnb_c", [128, 8])
    lng_c = sb("lng_c", [128, 8])
    st1 = sb("st1", [128, 1]); st2 = sb("st2", [128, 1])
    st3 = sb("st3", [128, 1]); st4 = sb("st4", [128, 1])
    qkt = sb("qkt", [128, 128])
    qt = sb("qt", [128, 128]); kt = sb("kt", [128, 128])
    ev = [sb(f"ev{i}", [128, 512]) for i in range(4)]
    b1tab = sb("b1tab", [128, HF // 128])
    bqt = sb("bqt", [128, D])
    qreg = sb("qreg", [128, 8])
    q_one, q_wm, q_iw, q_cve4, q_cvef, q_cb, q_ncb, q_tmp = \
        [qreg[:, i:i + 1] for i in range(8)]

    wba = sb("wba", [128, 8 * H2], BF16)
    wbb = sb("wbb", [128, 8 * H2], BF16)
    wmid = sb("wmid", [128, 12288])
    wqkh = sb("wqkh", [128, 8 * 128], BF16)
    wqkl = sb("wqkl", [128, 8 * 128], BF16)
    zbh = [sb("zbh0", [128, D], BF16), sb("zbh1", [128, D], BF16)]
    zbl = [sb("zbl0", [128, D], BF16), sb("zbl1", [128, D], BF16)]
    qth = sb("qth", [128, 128], BF16); qtl = sb("qtl", [128, 128], BF16)
    kth = sb("kth", [128, 128], BF16); ktl = sb("ktl", [128, 128], BF16)
    vsph = sb("vsph", [128, 512], BF16)
    vspl = sb("vspl", [128, 512], BF16)

    wstg = [wmid[:, 0:2048], wmid[:, 2048:4096]]
    wqks = wmid[:, 4096:5120]
    rowbuf2 = wmid[0:1, 5120:9216]
    lnbs = wmid[0:8, 9216:9344]
    lngs = wmid[0:8, 9344:9472]
    b1s = wmid[0:32, 9472:9600]
    rowbuf = wmid[0:1, 0:H2]
    biash = wmid[:, 0:4096]
    xin = wmid[:, 4096:6144]
    xmt_a = wmid[:, 6144:7168]
    zt_a = wmid[:, 7168:8192]
    qsrc = [wmid[:, 0:1024], wmid[:, 1024:2048]]
    qd_t0 = wmid[:, 2048:3072]
    qd_t3 = wmid[:, 3072:4096]
    qd_t12 = [(wmid[:, 4096:5120], wmid[:, 5120:6144]),
              (wmid[:, 6144:7168], wmid[:, 7168:8192])]
    xmt_d1 = [wmid[:, 10240:11264], wmid[:, 11264:12288]]
    w1st = [wmid[:, 8192:9216], wmid[:, 9216:10240]]
    x2h = [wba[:, 16384:20480], wba[:, 20480:24576]]
    x2l = [wbb[:, 16384:20480], wbb[:, 20480:24576]]
    hqb = [zbh[0], zbl[0]]
    hqld = [wbb[:, 0:4096], wbb[:, 4096:8192]]
    obuf_db = [wmid[:, 0:1024], wmid[:, 1024:2048]]
    oqbuf_db = [wmid[:, 2048:3072], wmid[:, 3072:4096]]
    u3_db = [wmid[:, 4096:5120], wmid[:, 5120:6144]]
    q2_t0 = wmid[:, 6144:7168]
    q2_t3 = wmid[:, 7168:8192]
    q2_t12 = [(wmid[:, 8192:9216], wmid[:, 9216:10240]),
              (wmid[:, 10240:11264], wmid[:, 11264:12288])]
    lnsq_d2 = wmid[:, 6144:7168]
    w2stB = wmid[:, 8192:10240]
    qt_wB = [wmid[:, 10240 + 512 * i:10240 + 512 * (i + 1)] for i in range(3)]
    wqb = [wba[:, 28672:30720], wba[:, 30720:32768]]
    qfh, kfh = wba[:, 0:N], wba[:, N:2 * N]
    qfl, kfl = wbb[:, 0:N], wbb[:, N:2 * N]
    aT = [wmid[:, 512 * j:512 * (j + 1)] for j in range(16)]
    aTh = [wba[:, 4096 + 512 * j:4096 + 512 * (j + 1)] for j in range(16)]
    aTl = [wbb[:, 4096 + 512 * j:4096 + 512 * (j + 1)] for j in range(16)]
    vgh = [wba[:, 12288:20480], wba[:, 20480:28672]]
    vgl = [wbb[:, 12288:20480], wbb[:, 20480:28672]]
    woh = wba[:, 0:16384]
    wol = wbb[:, 0:16384]
    oghm = [wba[:, 16384:18432], wba[:, 18432:20480]]
    oglm = [wbb[:, 16384:18432], wbb[:, 18432:20480]]
    ua_db = [wmid[:, 0:2048], wmid[:, 2048:4096]]
    gt_db = [wmid[:, 4096:6144], wmid[:, 6144:8192]]
    x_db = [wmid[:, 8192:9216], wmid[:, 9216:10240]]
    og_t = wmid[:, 10240:12288]
    u_t_c = og_t[:, 1024:2048]
    x2t_db = [gt_db[0][:, 0:1024], gt_db[1][:, 0:1024]]
    sq_db = [gt_db[0][:, 1024:2048], gt_db[1][:, 1024:2048]]
    wostg = [wmid[:, 0:1024], wmid[:, 1024:2048]]

    pall = es.enter_context(nc.psum_tensor("pall", [128, 4096], F32))
    PB = [pall[:, 512 * b:512 * (b + 1)] for b in range(8)]

    ps = {}

    with nc.Block() as block:
        free_at = [None] * 8
        rr = [0]

        def get_bank(te):
            b = rr[0]
            rr[0] = (rr[0] + 1) % 8
            if free_at[b] is not None:
                sem, cnt = free_at[b]
                te.wait_ge(sem.h, cnt)
            return b

        def bank_done(b, sem):
            free_at[b] = (sem, sem.count)

        def barrier():
            for opener in (block.sync, block.scalar, block.vector, block.tensor, block.gpsimd):
                def bodyf(e):
                    for sm in ALL_SEMS:
                        if sm.count > 0:
                            e.wait_ge(sm.h, sm.count)
                opener(bodyf)

        def VB(f):
            block.vector(lambda e: f(SE(e, s_v)))

        def SCB(f):
            block.scalar(lambda e: f(SE(e, s_s)))

        def GB(f):
            block.gpsimd(lambda e: f(SE(e, s_g)))

        def body(sync):
            sync.dma_start(out=lnbs, in_=bass.AP(lnb_h, 0, [[128, 8], [1, 128]])).then_inc(s_in.h, 16)
            sync.dma_start(out=lngs, in_=bass.AP(lng_h, 0, [[128, 8], [1, 128]])).then_inc(s_in.h, 16)
            sync.dma_start(out=b1s, in_=bass.AP(b1_h, 0, [[128, 32], [1, 128]])).then_inc(s_in.h, 16)
            sync.dma_start(out=g0[:], in_=bass.AP(osg_h, 0, [[1, 128], [1, 1]])).then_inc(s_in.h, 16)
            sync.dma_start(out=g1[:], in_=bass.AP(osg_h, 128, [[1, 128], [1, 1]])).then_inc(s_in.h, 16)
            sync.dma_start(out=be0[:], in_=bass.AP(osb_h, 0, [[1, 128], [1, 1]])).then_inc(s_in.h, 16)
            sync.dma_start(out=be1[:], in_=bass.AP(osb_h, 128, [[1, 128], [1, 1]])).then_inc(s_in.h, 16)
            sync.dma_start(out=bqk_eff[:], in_=bass.AP(bqk_h, 0, [[1, 128], [1, 1]])).then_inc(s_in.h, 16)
            sync.dma_start(out=rowbuf2, in_=bass.AP(bh_h, 0, [[H2, 1], [1, H2]])).then_inc(s_in.h, 16)
            for c in range(8):
                sync.dma_start(out=wqks[:, c * 128:(c + 1) * 128],
                               in_=wqk_h.ap()[c * 128:(c + 1) * 128, :]).then_inc(s_in.h, 16)
        block.sync(body)
        s_in.count += 16 * 17
        pre_loads = s_in.count

        def body(g):
            g.memset(eps_t[:], 1e-5)
            g.memset(ev[0][:, 0:448], 1.0)
            g.affine_select(rmask[:, 0:448], ev[0][:, 0:448], [[1, 448]], Alu.is_ge, 0.0,
                            base=-384, channel_multiplier=-1)
            g.memset(ev[0][:, 0:448], 1.0)
            g.affine_select(rmask[:, 448:896], ev[0][:, 0:448], [[1, 448]], Alu.is_ge, 0.0,
                            base=64, channel_multiplier=-1)
            g.memset(ev[1][:, 0:128], 1.0)
            g.affine_select(ident[:], ev[1][:, 0:128], [[1, 128]], Alu.is_equal, 0.0,
                            base=0, channel_multiplier=-1)
            g.iota(rsc2[:], [[128, NT]], base=1, channel_multiplier=1,
                   allow_small_or_imprecise_dtypes=True)
            g.memset(q_one, 1.0)
        GB(body)
        g_pre = s_g.count

        def body(v):
            v.wait_ge(s_g.h, g_pre)
            v.reciprocal(ev[2][:, 0:NT], rsc2[:])
            v.tensor_tensor(rsc2[:], ev[2][:, 0:NT], ev[2][:, 0:NT], Alu.mult)
            v.tensor_scalar(q_wm, q_one, C52, -C52, Alu.add, Alu.add)
            v.tensor_scalar(q_iw, q_wm, -1.0, 1.0, Alu.mult, Alu.add)
            v.tensor_scalar(q_cve4, q_iw, 2.0**20, None, Alu.mult)
            v.tensor_scalar(q_tmp, q_wm, 2.0**49, 1.0, Alu.mult, Alu.add)
            v.tensor_tensor(q_cve4, q_cve4, q_tmp, Alu.add)
            v.tensor_scalar(q_cvef, q_iw, 2.0**13, None, Alu.mult)
            v.tensor_scalar(q_tmp, q_wm, 2.0**42, 1.0, Alu.mult, Alu.add)
            v.tensor_tensor(q_cvef, q_cvef, q_tmp, Alu.add)
            v.tensor_scalar(q_cb, q_iw, C32, None, Alu.mult)
            v.tensor_scalar(q_tmp, q_wm, C52, None, Alu.mult)
            v.tensor_tensor(q_cb, q_cb, q_tmp, Alu.add)
            v.tensor_scalar(q_ncb, q_cb, -1.0, None, Alu.mult)
        VB(body)

        def body(t):
            t.wait_ge(s_in.h, pre_loads)
            t.wait_ge(s_g.h, g_pre)
            t.transpose(pall[:, 0:8], lnbs, ident[0:8, 0:8])
            t.transpose(pall[:, 8:16], lngs, ident[0:8, 0:8])
            ins = t.transpose(pall[:, 16:48], b1s, ident[0:32, 0:32])
            s_mm.inc(ins)
        block.tensor(body)
        tp3 = s_mm.count

        def body(v):
            v.wait_ge(s_mm.h, tp3)
            v.tensor_copy(lnb_c[:], pall[:, 0:8])
            v.tensor_copy(lng_c[:], pall[:, 8:16])
            v.tensor_copy(b1tab[:], pall[:, 16:48])
        VB(body)
        lnbc_ready = s_v.count

        def body(t):
            t.wait_ge(s_in.h, pre_loads)
            t.wait_ge(s_v.h, lnbc_ready)
            for c in range(8):
                ins = t.matmul(pall[:, 0:1], wqks[:, 128 * c:128 * (c + 1)], lnb_c[:, c:c + 1],
                               start=(c == 0), stop=(c == 7))
            s_mm.inc(ins)
        block.tensor(body)
        bqk_mm = s_mm.count

        def body(v):
            v.wait_ge(s_mm.h, bqk_mm)
            v.tensor_tensor(bqk_eff[:], bqk_eff[:], pall[:, 0:1], Alu.add)
            for c in range(8):
                v.tensor_scalar(wqks[:, c * 128:(c + 1) * 128], wqks[:, c * 128:(c + 1) * 128],
                                lng_c[:, c:c + 1], None, Alu.mult)
            v.tensor_copy(wqkh[:], wqks[:])
            v.tensor_tensor(wqkl[:], wqks[:], wqkh[:], Alu.subtract)
        VB(body)
        bqk_done = s_v.count

        wh_split = {}
        for ch in range(16):
            c, half = divmod(ch, 2)
            stg = wstg[ch % 2]
            sw = s_w1[ch % 2]

            def body(sync, c=c, half=half, ch=ch, stg=stg, sw=sw):
                if ch >= 2:
                    sync.wait_ge(s_v.h, wh_split[ch - 2])
                ins = sync.dma_start(out=stg, in_=wh_h.ap()[c * 128:(c + 1) * 128,
                                                            2048 * half:2048 * (half + 1)])
                sw.dma(ins)
            block.sync(body)
            wh_loaded = sw.count

            def body(t, c=c, half=half, stg=stg, sw=sw, wh_loaded=wh_loaded):
                t.wait_ge(sw.h, wh_loaded)
                if c == 0 and half == 0:
                    t.wait_ge(s_v.h, bqk_done)
                for bb in range(4):
                    ins = t.matmul(pall[0:1, 2048 * half + 512 * bb:2048 * half + 512 * (bb + 1)],
                                   lnb_c[:, c:c + 1], stg[:, 512 * bb:512 * (bb + 1)],
                                   start=(c == 0), stop=(c == 7))
                s_mm.inc(ins)
            block.tensor(body)
            wh_mm = s_mm.count

            def body(v, c=c, half=half, stg=stg, wh_mm=wh_mm):
                v.wait_ge(s_mm.h, wh_mm)
                v.tensor_scalar(stg, stg, lng_c[:, c:c + 1], None, Alu.mult)
                dst = slice(c * H2 + 2048 * half, c * H2 + 2048 * (half + 1))
                v.tensor_copy(wba[:, dst], stg)
                v.tensor_tensor(wbb[:, dst], stg, wba[:, dst], Alu.subtract)
            VB(body)
            wh_split[ch] = s_v.count
        a_wready = s_v.count

        def body(v):
            v.wait_ge(s_mm.h, s_mm.count)
            v.wait_ge(s_in.h, pre_loads)
            v.tensor_tensor(rowbuf2, pall[0:1, :], rowbuf2, Alu.add)
        VB(body)
        rowadd_v = s_v.count
        for b in range(8):
            free_at[b] = (s_v, s_v.count)

        def body(s_):
            s_.wait_ge(s_v.h, rowadd_v)
            ins = s_.dma_start(out=sc_row.ap()[0:1, :], in_=rowbuf2)
            s_row0.dma(ins)
        block.scalar(body)

        def body(sync):
            sync.wait_ge(s_row0.h, s_row0.count)
            sync.wait_ge(s_v.h, a_wready)
            ins = sync.dma_start(out=biash, in_=bass.AP(sc_row, 0, [[0, 128], [1, H2]]))
            s_bias.dma(ins)
        block.sync(body)

        def ln_tile(width, src_ap, dst_ap, xm, sq, rsqrt_scalar=False):
            def body(v):
                v.reduce_sum(st1[:], src_ap, axis=AX.X)
                v.tensor_scalar(st2[:], st1[:], 1.0 / width, None, Alu.mult)
                v.tensor_scalar(xm, src_ap, st2[:], None, Alu.subtract)
                v.scalar_tensor_tensor(sq, xm, 1.0, xm, Alu.mult, Alu.mult,
                                       accum_out=st3[:])
            VB(body)

            if rsqrt_scalar:
                def body(s_):
                    s_.wait_ge(s_v.h, s_v.count)
                    s_.activation(st4[:], st3[:], Act.Ln, bias=eps_t[:], scale=1.0 / width)
                    s_.activation(st4[:], st4[:], Act.Exp, scale=-0.5)
                SCB(body)

                def body(v):
                    v.wait_ge(s_s.h, s_s.count)
                    v.tensor_scalar(dst_ap, xm, st4[:], None, Alu.mult)
                VB(body)
                return

            def body(s_):
                s_.wait_ge(s_v.h, s_v.count)
                s_.activation(st4[:], st3[:], Act.Sqrt, bias=eps_t[:], scale=1.0 / width)
            SCB(body)

            def body(v):
                v.wait_ge(s_s.h, s_s.count)
                v.reciprocal(st1[:], st4[:])
                v.tensor_scalar(dst_ap, xm, st1[:], None, Alu.mult)
            VB(body)

        def emit_quant2(src, dst, t0, t1, t2, t3, mode, gates=(), ggates=()):
            if mode == "e4m3":
                cve, SSC, SISC, THRSQ, OVT2 = q_cve4, 512.0, 2.0**-9, 2.0**-12, 61000.0
            else:
                cve, SSC, SISC, THRSQ, OVT2 = q_cvef, 2.0**24, 2.0**-24, 2.0**-28, 4292870144.0

            def gbody(g):
                for sem, cnt in ggates:
                    g.wait_ge(sem.h, cnt)
                for sem, cnt in gates:
                    g.wait_ge(sem.h, cnt)
                g.tensor_scalar(t1, src, SSC, q_cb[:], Alu.mult, Alu.add)
                g.tensor_scalar(t1, t1, q_ncb[:], SISC, Alu.add, Alu.mult)
                g.tensor_tensor(t2, src, src, Alu.mult)
                g.tensor_scalar(t2, t2, THRSQ, None, Alu.is_lt)
            GB(gbody)
            gp_done = s_g.count

            def body(v):
                for sem, cnt in gates:
                    v.wait_ge(sem.h, cnt)
                v.tensor_scalar(t0, src, cve, None, Alu.mult)
                v.tensor_tensor(t3, t0, src, Alu.subtract)
                v.tensor_tensor(t0, t0, t3, Alu.subtract)
                v.wait_ge(s_g.h, gp_done)
                v.tensor_tensor(t3, t1, t0, Alu.subtract)
                v.tensor_tensor(t3, t3, t2, Alu.mult)
                v.tensor_tensor(t0, t0, t3, Alu.add)
                v.tensor_tensor(t3, t0, t0, Alu.mult)
                v.tensor_scalar(t3, t3, OVT2, BIG, Alu.is_gt, Alu.mult)
                v.scalar_tensor_tensor(dst, t3, 1.0, t0, Alu.add, Alu.mult)
            VB(body)

        def emit_quant(src, dst, t0, t1, t2, mode, gates=(), rows=128,
                       unsigned=False, sub_scalar=False, dst_gates=()):
            if mode == "e4m3":
                cve, SSC, SISC, THRSQ, OVT2 = q_cve4, 512.0, 2.0**-9, 2.0**-12, 61000.0
            else:
                cve, SSC, SISC, THRSQ, OVT2 = q_cvef, 2.0**24, 2.0**-24, 2.0**-28, 4292870144.0
            cve, cb, ncb = cve[0:rows, :], q_cb[0:rows, :], q_ncb[0:rows, :]

            if unsigned and sub_scalar is None:
                thr_m = float(2.0 ** round(math.log2(THRSQ) / 2))
                thr_ov = 248.0 if mode == "e4m3" else 65520.0

                def bodyA(v):
                    for sem, cnt in gates:
                        v.wait_ge(sem.h, cnt)
                    v.tensor_scalar(t0, src, cve, None, Alu.mult)
                    v.tensor_tensor(t2, t0, src, Alu.subtract)
                    v.tensor_tensor(t0, t0, t2, Alu.subtract)
                    v.tensor_scalar(t1, src, SSC, cb, Alu.mult, Alu.add)
                    v.tensor_scalar(t2, src, thr_m, None, Alu.is_lt)
                VB(bodyA)
                a_done = s_v.count

                def bodyB(v):
                    for sem, cnt in dst_gates:
                        v.wait_ge(sem.h, cnt)
                    v.tensor_scalar(t1, t1, ncb, SISC, Alu.add, Alu.mult)
                    v.tensor_tensor(t1, t1, t0, Alu.subtract)
                    v.tensor_tensor(t1, t1, t2, Alu.mult)
                    v.tensor_tensor(t0, t0, t1, Alu.add)
                    v.tensor_scalar(t2, t0, thr_ov, BIG, Alu.is_gt, Alu.mult)
                    v.scalar_tensor_tensor(dst, t2, 1.0, t0, Alu.add, Alu.mult)
                VB(bodyB)
                return a_done

            if sub_scalar:
                def sbody(s_):
                    for sem, cnt in gates:
                        s_.wait_ge(sem.h, cnt)
                    s_.activation(t1, src, Act.Identity, bias=cb, scale=SSC)
                SCB(sbody)
                sgates = [(s_s, s_s.count)]

            def body(v):
                for sem, cnt in gates:
                    v.wait_ge(sem.h, cnt)
                if sub_scalar:
                    for sem, cnt in sgates:
                        v.wait_ge(sem.h, cnt)
                v.tensor_scalar(t0, src, cve, None, Alu.mult)
                v.tensor_tensor(t1 if not sub_scalar else t2, t0, src, Alu.subtract)
                v.tensor_tensor(t0, t0, t1 if not sub_scalar else t2, Alu.subtract)
                if not sub_scalar:
                    v.tensor_scalar(t1, src, SSC, cb, Alu.mult, Alu.add)
                v.tensor_scalar(t1, t1, ncb, SISC, Alu.add, Alu.mult)
                if unsigned:
                    v.tensor_scalar(t2, src, float(2.0 ** round(math.log2(THRSQ) / 2)),
                                    None, Alu.is_lt)
                else:
                    v.tensor_tensor(t2, src, src, Alu.mult)
                    v.tensor_scalar(t2, t2, THRSQ, None, Alu.is_lt)
                v.tensor_tensor(t1, t1, t0, Alu.subtract)
                v.tensor_tensor(t1, t1, t2, Alu.mult)
                v.tensor_tensor(t0, t0, t1, Alu.add)
                if unsigned:
                    thr = 248.0 if mode == "e4m3" else 65520.0
                    v.tensor_scalar(t2, t0, thr, BIG, Alu.is_gt, Alu.mult)
                else:
                    v.tensor_tensor(t2, t0, t0, Alu.mult)
                    v.tensor_scalar(t2, t2, OVT2, BIG, Alu.is_gt, Alu.mult)
                v.scalar_tensor_tensor(dst, t2, 1.0, t0, Alu.add, Alu.mult)
            VB(body)

        ln_done = {}
        a_mm_done = {}
        a_tb = {}
        a_tdone = {}
        zfm = {}

        def a_front(t):
            s, ti = divmod(t, NT)
            xb = xin[:, (t % 2) * D:(t % 2 + 1) * D]
            sx = s_xa[t % 2]

            def body(sync, t=t, s=s, ti=ti, xb=xb, sx=sx):
                if t >= 2:
                    sync.wait_ge(s_v.h, ln_done[t - 2])
                else:
                    sync.wait_ge(s_v.h, rowadd_v)
                    sync.wait_ge(s_row0.h, s_row0.count)
                ins = sync.dma_start(out=xb, in_=x_h.ap()[s, 128 * ti:128 * (ti + 1), :])
                sx.dma(ins)
            block.sync(body)
            x_loaded = sx.count

            def body(v, t=t, sx=sx, x_loaded=x_loaded):
                v.wait_ge(sx.h, x_loaded)
                if t == 0:
                    v.wait_ge(s_row0.h, s_row0.count)
                    v.wait_ge(s_bias.h, s_bias.count)
            block.vector(body)
            ln_tile(D, xb, zt_a, xmt_a, zt_a)
            ln_done[t] = s_v.count

        def a_transp(t):
            tb = []
            a_tb[t] = tb

            def body(te, zr=ln_done[t]):
                te.wait_ge(s_v.h, zr)
                for half in range(2):
                    b = get_bank(te)
                    tb.append(b)
                    for cc in range(4):
                        c = 4 * half + cc
                        ins = te.transpose(PB[b][:, 128 * cc:128 * (cc + 1)],
                                           zt_a[:, 128 * c:128 * (c + 1)], ident[:])
                    s_mm.inc(ins)
            block.tensor(body)
            a_tdone[t] = s_mm.count

        def a_evac_qk(t):
            s, ti = divmod(t, NT)
            tb = a_tb[t]
            zp = t % 2

            def body(v, t=t, tb=tb, zp=zp, tdone=a_tdone[t]):
                v.wait_ge(s_mm.h, tdone)
                if t > 1:
                    v.wait_ge(s_mm.h, a_mm_done[t - 2])
                for half in range(2):
                    sl = slice(512 * half, 512 * (half + 1))
                    v.tensor_copy(zbh[zp][:, sl], PB[tb[half]][:])
                    v.tensor_tensor(zbl[zp][:, sl], PB[tb[half]][:], zbh[zp][:, sl],
                                    Alu.subtract)
            VB(body)
            for half in range(2):
                bank_done(tb[half], s_v)
            zfm[t] = zfm_ready = s_v.count

            qb = []
            def body(te, zp=zp, zfm_ready=zfm_ready):
                te.wait_ge(s_v.h, zfm_ready)
                qb.append(get_bank(te))
                for c in range(8):
                    cs = slice(128 * c, 128 * (c + 1))
                    te.matmul(PB[qb[0]][:, 0:128], wqkh[:, cs], zbh[zp][:, cs],
                              start=(c == 0), stop=False)
                    te.matmul(PB[qb[0]][:, 0:128], wqkh[:, cs], zbl[zp][:, cs],
                              start=False, stop=False)
                    ins = te.matmul(PB[qb[0]][:, 0:128], wqkl[:, cs], zbh[zp][:, cs],
                                    start=False, stop=(c == 7))
                s_mm.inc(ins)
            block.tensor(body)
            qk_mm = s_mm.count

            def body(s_, qk_mm=qk_mm, qb=qb):
                s_.wait_ge(s_mm.h, qk_mm)
                s_.activation(qkt[:], PB[qb[0]][:, 0:128], Act.Silu, bias=bqk_eff[:])
            SCB(body)
            qkt_done = s_s.count

            def body(v, t=t, qkt_done=qkt_done):
                v.wait_ge(s_s.h, qkt_done)
                v.wait_ge(s_qts.h, 64 * t)
                v.tensor_scalar(qt[:], qkt[:], g0[:], be0[:], Alu.mult, Alu.add)
                v.tensor_scalar(kt[:], qkt[:], g1[:], be1[:], Alu.mult, Alu.add)
                v.tensor_copy(qth[:], qt[:])
                v.tensor_tensor(qtl[:], qt[:], qth[:], Alu.subtract)
                v.tensor_copy(kth[:], kt[:])
                v.tensor_tensor(ktl[:], kt[:], kth[:], Alu.subtract)
            VB(body)
            bank_done(qb[0], s_v)
            qk_done = s_v.count

            def body(s_, s=s, ti=ti, qk_done=qk_done):
                s_.wait_ge(s_v.h, qk_done)
                cs = slice(128 * ti, 128 * (ti + 1))
                for src, dst in ((qth, sc_qh), (qtl, sc_ql), (kth, sc_kh), (ktl, sc_kl)):
                    ins = s_.dma_start(out=dst.ap()[s, :, cs], in_=src[:])
                    s_qts.dma(ins)
            block.scalar(body)

        def a_cb(t, cb):
            s, ti = divmod(t, NT)
            hb = []
            zp = t % 2

            def body(te, cb=cb, zp=zp, zfm_ready=zfm[t]):
                te.wait_ge(s_v.h, zfm_ready)
                if cb == 0:
                    te.wait_ge(s_bias.h, s_bias.count)
                hb.append(get_bank(te))
                for c in range(8):
                    zs = slice(128 * c, 128 * (c + 1))
                    ws = slice(c * H2 + 512 * cb, c * H2 + 512 * (cb + 1))
                    te.matmul(PB[hb[0]][:], zbh[zp][:, zs], wba[:, ws],
                              start=(c == 0), stop=False)
                    te.matmul(PB[hb[0]][:], zbh[zp][:, zs], wbb[:, ws],
                              start=False, stop=False)
                    ins = te.matmul(PB[hb[0]][:], zbl[zp][:, zs], wba[:, ws],
                                    start=False, stop=(c == 7))
                s_mm.inc(ins)
            block.tensor(body)
            this_mm = s_mm.count
            if cb < 4:
                eb = ev[cb % 2][:]
                seb = None
            else:
                eb = ev[2 + cb % 2][:]
                seb = s_ev[2 + cb % 2]

            def body(v, cb=cb, eb=eb, seb=seb, hb=hb, this_mm=this_mm):
                v.wait_ge(s_mm.h, this_mm)
                if seb is not None:
                    v.wait_ge(seb.h, seb.count)
                v.tensor_tensor(eb, PB[hb[0]][:], biash[:, 512 * cb:512 * (cb + 1)], Alu.add)
            VB(body)
            bank_done(hb[0], s_v)

            def body(s_, eb=eb):
                s_.wait_ge(s_v.h, s_v.count)
                s_.activation(eb, eb, Act.Silu)
            SCB(body)
            silu_done = s_s.count

            if cb < 4:
                def body(v, eb=eb, silu_done=silu_done):
                    v.wait_ge(s_s.h, silu_done)
                    v.wait_ge(s_ev[0].h, s_ev[0].count)
                    v.wait_ge(s_ev[1].h, s_ev[1].count)
                    v.tensor_copy(vsph[:], eb)
                    v.tensor_tensor(vspl[:], eb, vsph[:], Alu.subtract)
                VB(body)
                vsplit_done = s_v.count

                def body(s_, s=s, ti=ti, cb=cb, vsplit_done=vsplit_done):
                    s_.wait_ge(s_v.h, vsplit_done)
                    rs = slice(128 * ti, 128 * (ti + 1))
                    ins = s_.dma_start(out=sc_vh.ap()[s, rs, 512 * cb:512 * (cb + 1)],
                                       in_=vsph[:])
                    s_ev[0].dma(ins)
                    ins = s_.dma_start(out=sc_vl.ap()[s, rs, 512 * cb:512 * (cb + 1)],
                                       in_=vspl[:])
                    s_ev[1].dma(ins)
                block.scalar(body)
            else:
                def body(s_, s=s, ti=ti, cb=cb, eb=eb, seb=seb, silu_done=silu_done):
                    s_.wait_ge(s_s.h, silu_done)
                    col = 512 * (cb - 4)
                    ins = s_.dma_start(out=sc_g.ap()[s, 128 * ti:128 * (ti + 1),
                                                     col:col + 512], in_=eb)
                    seb.dma(ins)
                block.scalar(body)

        a_front(0)
        a_transp(0)
        a_evac_qk(0)
        for t in range(TT):
            for cb in range(8):
                a_cb(t, cb)
                if cb == 0 and t + 1 < TT:
                    a_front(t + 1)
                if cb == 3 and t + 1 < TT:
                    a_transp(t + 1)
                if cb == 5 and t + 1 < TT:
                    a_evac_qk(t + 1)
            a_mm_done[t] = s_mm.count

        barrier()

        w_cursor = [0]
        w_loaded = {}

        def emit_wload(i):
            if i >= 16:
                return
            c, hf = divmod(i, 2)

            def body(g, i=i, c=c, hf=hf):
                if i >= 1:
                    g.wait_ge(s_v.h, ps[f"wq_{i - 1}"])
                else:
                    g.wait_ge(s_v.h, lnbc_ready)
                    g.wait_ge(s_row0.h, s_row0.count)
                ins = g.dma_start(out=w2stB,
                                  in_=w2_h.ap()[128 * c:128 * (c + 1),
                                                2048 * hf:2048 * (hf + 1)])
                s_wcg.dma(ins)
            block.gpsimd(body)
            w_loaded[i] = s_wcg.count

        def emit_wpieces(n):
            for _ in range(n):
                k = w_cursor[0]
                if k >= 64:
                    return
                w_cursor[0] = k + 1
                i, sub = divmod(k, 4)
                sws = s_ws[i % 2]
                gates = ([(s_wcg, w_loaded[i]), (sws, sws.count)]
                         if sub == 0 else ())
                emit_quant(w2stB[:, 512 * sub:512 * (sub + 1)],
                           wqb[i % 2][:, 512 * sub:512 * (sub + 1)],
                           qt_wB[0], qt_wB[1], qt_wB[2], "e4m3", gates=gates)
                if sub == 3:
                    ps[f"wq_{i}"] = s_v.count
                    c, hf = divmod(i, 2)

                    def body(g, i=i, c=c, hf=hf, sws=sws):
                        g.wait_ge(s_v.h, ps[f"wq_{i}"])
                        ins = g.dma_start(
                            out=bass.AP(sc_wq2, 128 * c * HF + 2048 * hf,
                                        [[HF, 128], [1, 2048]]),
                            in_=wqb[i % 2][:])
                        sws.dma(ins)
                    block.gpsimd(body)
                    emit_wload(i + 1)

        tbanks = []
        ps["aT_consumed"] = 0
        emit_wload(0)

        for s in range(SEQ):
            def body(sync, s=s):
                sync.wait_ge(s_mm.h, s_mm.count)
                for src, dst in ((sc_qh, qfh), (sc_ql, qfl), (sc_kh, kfh)):
                    sync.dma_start(out=dst, in_=src.ap()[s, :, :]).then_inc(s_bqk.h, 16)
                ins = sync.dma_start(out=kfl, in_=sc_kl.ap()[s, :, :])
                s_bqk.dma(ins)
            block.sync(body)
            s_bqk.count += 48
            qk_loaded = s_bqk.count

            for g in range(4):
                nk = 4 * g + 4
                for j in range(nk):
                    def body(te, j=j, g=g, qk_loaded=qk_loaded):
                        te.wait_ge(s_bqk.h, qk_loaded)
                        b = get_bank(te)
                        tbanks.append(b)
                        js = slice(128 * j, 128 * (j + 1))
                        gs = slice(512 * g, 512 * (g + 1))
                        te.matmul(PB[b][:], kfh[:, js], qfh[:, gs], start=True, stop=False)
                        te.matmul(PB[b][:], kfh[:, js], qfl[:, gs], start=False, stop=False)
                        ins = te.matmul(PB[b][:], kfl[:, js], qfh[:, gs],
                                        start=False, stop=True)
                        s_mm.inc(ins)
                    block.tensor(body)
                    this_mm = s_mm.count
                    sbank = tbanks[-1]

                    def body(s_, j=j, sbank=sbank, this_mm=this_mm):
                        s_.wait_ge(s_mm.h, this_mm)
                        s_.wait_ge(s_mm.h, ps["aT_consumed"])
                        s_.activation(aT[j], PB[sbank][:], Act.Relu)
                    SCB(body)

                    def body(v, j=j, g=g):
                        v.wait_ge(s_s.h, s_s.count)
                        if j >= 4 * g:
                            off = 384 - (128 * j - 512 * g)
                            v.tensor_tensor(aT[j], aT[j], rmask[:, off:off + 512], Alu.mult)
                        v.tensor_tensor(aT[j], aT[j], aT[j], Alu.mult)
                        v.tensor_copy(aTh[j], aT[j])
                        v.tensor_tensor(aTl[j], aT[j], aTh[j], Alu.subtract)
                    VB(body)
                    bank_done(sbank, s_v)
                    ps[f"aT_split_{j}"] = s_v.count
                tbanks.clear()
                simT_done = s_v.count

                for dq in range(4):
                    vgbh = vgh[dq % 2]
                    vgbl = vgl[dq % 2]
                    svg = s_vg[dq % 2]

                    def body(sync, s=s, dq=dq, nk=nk, vgbh=vgbh, vgbl=vgbl, svg=svg):
                        sync.wait_ge(s_mm.h, ps.get(f"vg_used_{dq % 2}", 0))
                        for j in range(nk):
                            js = slice(512 * j, 512 * (j + 1))
                            src = (s, slice(128 * j, 128 * (j + 1)),
                                   slice(512 * dq, 512 * (dq + 1)))
                            sync.dma_start(out=vgbh[:, js],
                                           in_=sc_vh.ap()[src]).then_inc(svg.h, 16)
                            ins = sync.dma_start(out=vgbl[:, js], in_=sc_vl.ap()[src])
                            svg.dma(ins)
                            svg.count += 16
                    block.sync(body)
                    vg_loaded = svg.count

                    obanks = []
                    splits = [ps[f"aT_split_{j}"] for j in range(nk)]
                    def body(te, g=g, nk=nk, vgbh=vgbh, vgbl=vgbl, svg=svg,
                             vg_loaded=vg_loaded, splits=splits):
                        te.wait_ge(svg.h, vg_loaded)
                        for it in range(4):
                            obanks.append(get_bank(te))
                        for j in range(nk):
                            te.wait_ge(s_v.h, splits[j])
                            js = slice(512 * j, 512 * (j + 1))
                            for it in range(max(0, j - 4 * g), 4):
                                its = slice(128 * it, 128 * (it + 1))
                                te.matmul(PB[obanks[it]][:], aTh[j][:, its], vgbh[:, js],
                                          start=(j == 0), stop=False)
                                te.matmul(PB[obanks[it]][:], aTh[j][:, its], vgbl[:, js],
                                          start=False, stop=False)
                                ins = te.matmul(PB[obanks[it]][:], aTl[j][:, its],
                                                vgbh[:, js], start=False,
                                                stop=(j == 4 * g + it))
                        s_mm.inc(ins)
                    block.tensor(body)
                    ps[f"vg_used_{dq % 2}"] = s_mm.count
                    av_mm = s_mm.count

                    emit_wpieces({1: 2, 2: 3, 3: 3}.get(g, 0))

                    for it in range(4):
                        seb = s_ev[it]

                        def body(v, it=it, g=g, seb=seb, av_mm=av_mm):
                            v.wait_ge(s_mm.h, av_mm)
                            v.wait_ge(seb.h, seb.count)
                            v.tensor_scalar(ev[it][:], PB[obanks[it]][:],
                                            rsc2[:, 4 * g + it:4 * g + it + 1], None, Alu.mult)
                        VB(body)
                        bank_done(obanks[it], s_v)

                        def body(s_, s=s, g=g, it=it, dq=dq, seb=seb):
                            s_.wait_ge(s_v.h, s_v.count)
                            row = 128 * (4 * g + it)
                            ins = s_.dma_start(out=sc_ua.ap()[s, row:row + 128, 512 * dq:512 * (dq + 1)],
                                               in_=ev[it][:])
                            seb.dma(ins)
                        block.scalar(body)
                ps["aT_consumed"] = s_mm.count
        emit_wpieces(64)

        barrier()

        for c in range(16):
            swo = s_c[c % 2]

            def body(sync, c=c, swo=swo):
                if c >= 2:
                    sync.wait_ge(s_v.h, ps[f"wo_split_{c - 2}"])
                ins = sync.dma_start(out=wostg[c % 2],
                                     in_=wo_h.ap()[128 * c:128 * (c + 1), :])
                swo.dma(ins)
                if c == 0:
                    ins = sync.dma_start(out=bqt[:], in_=bass.AP(bo_h, 0, [[0, 128], [1, D]]))
                    s_wo.dma(ins)
            block.sync(body)
            wo_c = swo.count

            def body(v, c=c, swo=swo, wo_c=wo_c):
                v.wait_ge(swo.h, wo_c)
                if c == 0:
                    v.wait_ge(s_wo.h, s_wo.count)
                dst = slice(1024 * c, 1024 * (c + 1))
                v.tensor_copy(woh[:, dst], wostg[c % 2])
                v.tensor_tensor(wol[:, dst], wostg[c % 2], woh[:, dst], Alu.subtract)
            VB(body)
            ps[f"wo_split_{c}"] = s_v.count
        wo_ready = s_v.count

        c_used = {}

        def emit_c_ln(tp):
            dbp = tp % 2
            sp, tip = divmod(tp, NT)
            sxpp = s_x2sp if dbp == 0 else s_x2b
            src = x_db[dbp]

            def body(v, src=src):
                v.wait_ge(s_x2sp.h, s_x2sp.count)
                v.wait_ge(s_x2b.h, s_x2b.count)
                v.reduce_sum(st1[:], src, axis=AX.X)
                v.tensor_scalar(st2[:], st1[:], 1.0 / D, None, Alu.mult)
                for hh in range(2):
                    hs = slice(512 * hh, 512 * (hh + 1))
                    v.tensor_scalar(ev[hh][:], src[:, hs], st2[:], None, Alu.subtract)
                    v.scalar_tensor_tensor(ev[2 + hh][:], ev[hh][:], 1.0, ev[hh][:],
                                           Alu.mult, Alu.mult,
                                           accum_out=(st3 if hh == 0 else st4)[:])
                v.tensor_tensor(st3[:], st3[:], st4[:], Alu.add)
            VB(body)
            c_used[tp] = s_v.count

            def body(s_):
                s_.wait_ge(s_v.h, s_v.count)
                s_.activation(st4[:], st3[:], Act.Ln, bias=eps_t[:], scale=1.0 / D)
                s_.activation(st4[:], st4[:], Act.Exp, scale=-0.5)
            SCB(body)

            def body(v):
                v.wait_ge(s_s.h, s_s.count)
                for hh in range(2):
                    v.tensor_scalar(ev[hh][:], ev[hh][:], st4[:], None, Alu.mult)
            VB(body)

            def body(s_, sp=sp, tip=tip, sxpp=sxpp):
                s_.wait_ge(s_v.h, s_v.count)
                for hh in range(2):
                    ins = s_.dma_start(
                        out=sc_x2.ap()[sp, 128 * tip:128 * (tip + 1),
                                       512 * hh:512 * (hh + 1)],
                        in_=ev[hh][:])
                    sxpp.dma(ins)
            block.scalar(body)

        c_loaded = {}
        c_xloaded = {}

        def c_loads(t):
            s, ti = divmod(t, NT)
            db = t % 2
            sc = s_c[db]

            def body(sync, s=s, ti=ti, db=db, t=t, sc=sc):
                if t >= 2:
                    sync.wait_ge(s_v.h, ps[f"c_og_{t - 2}"])
                else:
                    sync.wait_ge(s_v.h, wo_ready)
                sync.dma_start(out=ua_db[db], in_=sc_ua.ap()[s, 128 * ti:128 * (ti + 1), :]).then_inc(sc.h, 16)
                ins = sync.dma_start(out=gt_db[db], in_=sc_g.ap()[s, 128 * ti:128 * (ti + 1), :])
                sc.dma(ins)
            block.sync(body)
            sc.count += 16
            c_loaded[t] = sc.count

        def c_xload(t):
            s, ti = divmod(t, NT)
            db = t % 2
            sxp = s_x2sp if db == 0 else s_x2b
            sxl = s_xd1 if db == 0 else s_xd1b

            def body(sync, s=s, ti=ti, db=db, t=t, sxp=sxp, sxl=sxl):
                if t >= 2:
                    sync.wait_ge(s_v.h, c_used[t - 2])
                    sync.wait_ge(sxp.h, 32 * (t // 2))
                ins = sync.dma_start(out=x_db[db], in_=x_h.ap()[s, 128 * ti:128 * (ti + 1), :])
                sxl.dma(ins)
            block.sync(body)
            c_xloaded[t] = (sxl, sxl.count)

        def c_og(t):
            db = t % 2
            sc = s_c[db]

            def body(v, db=db, sc=sc, cl=c_loaded[t],
                     ogfree=ps.get("c_tdone", 0)):
                v.wait_ge(sc.h, cl)
                v.wait_ge(s_mm.h, ogfree)
                v.tensor_tensor(og_t, ua_db[db], gt_db[db], Alu.mult)
            VB(body)
            ps[f"c_og_{t}"] = s_v.count

        c_tb = {}

        def c_transp(t):
            tb = []
            c_tb[t] = tb

            def body(te, og_ready=ps[f"c_og_{t}"]):
                te.wait_ge(s_v.h, og_ready)
                for qtr in range(4):
                    b = get_bank(te)
                    tb.append(b)
                    for cc in range(4):
                        c = 4 * qtr + cc
                        ins = te.transpose(PB[b][:, 128 * cc:128 * (cc + 1)],
                                           og_t[:, 128 * c:128 * (c + 1)], ident[:])
                    s_mm.inc(ins)
            block.tensor(body)
            ps["c_tdone"] = ps[f"c_td_{t}"] = s_mm.count

        def c_evac(t):
            tb = c_tb[t]
            db = t % 2

            def body(v, tb=tb, db=db, tdone=ps[f"c_td_{t}"]):
                v.wait_ge(s_mm.h, tdone)
                for qtr in range(4):
                    sl = slice(512 * qtr, 512 * (qtr + 1))
                    v.tensor_copy(oghm[db][:, sl], PB[tb[qtr]][:])
                    v.tensor_tensor(oglm[db][:, sl], PB[tb[qtr]][:], oghm[db][:, sl],
                                    Alu.subtract)
            VB(body)
            for qtr in range(4):
                bank_done(tb[qtr], s_v)
            ps[f"c_ogfm_{t}"] = s_v.count

        c_ob = {}

        def c_mm_half(t, b2):
            db = t % 2
            if b2 == 0:
                c_ob[t] = []
            ob2 = c_ob[t]

            def body(te, b2=b2, db=db, ogfm_ready=ps[f"c_ogfm_{t}"]):
                te.wait_ge(s_v.h, ogfm_ready)
                b = get_bank(te)
                ob2.append(b)
                for c in range(16):
                    ogs = slice(128 * c, 128 * (c + 1))
                    ws = slice(1024 * c + 512 * b2, 1024 * c + 512 * (b2 + 1))
                    te.matmul(PB[b][:], oghm[db][:, ogs], woh[:, ws],
                              start=(c == 0), stop=False)
                    te.matmul(PB[b][:], oghm[db][:, ogs], wol[:, ws],
                              start=False, stop=False)
                    ins = te.matmul(PB[b][:], oglm[db][:, ogs], woh[:, ws],
                                    start=False, stop=(c == 15))
                s_mm.inc(ins)
            block.tensor(body)
            ps[f"c_mm_{t}"] = s_mm.count

        def c_u(t):
            db = t % 2
            ob2 = c_ob[t]

            def body(v, db=db, mm2=ps[f"c_mm_{t}"], ob2=ob2, xl=c_xloaded[t]):
                v.wait_ge(s_mm.h, mm2)
                v.wait_ge(xl[0].h, xl[1])
                for b2 in range(2):
                    v.scalar_tensor_tensor(x_db[db][:, 512 * b2:512 * (b2 + 1)],
                                           x_db[db][:, 512 * b2:512 * (b2 + 1)], 2.0,
                                           PB[ob2[b2]][:], Alu.mult, Alu.add)
                    v.tensor_tensor(x_db[db][:, 512 * b2:512 * (b2 + 1)],
                                    x_db[db][:, 512 * b2:512 * (b2 + 1)],
                                    bqt[:, 512 * b2:512 * (b2 + 1)], Alu.add)
            VB(body)
            for b2 in range(2):
                bank_done(ob2[b2], s_v)

        c_loads(0)
        c_xload(0)
        c_og(0)
        c_transp(0)
        c_evac(0)
        for t in range(TT):
            if t + 1 < TT:
                c_loads(t + 1)
            if t > 0:
                emit_c_ln(t - 1)
            if t + 1 < TT:
                c_og(t + 1)
                c_xload(t + 1)
            c_mm_half(t, 0)
            if t + 1 < TT:
                c_transp(t + 1)
            c_mm_half(t, 1)
            if t + 1 < TT:
                c_evac(t + 1)
            c_u(t)
        emit_c_ln(TT - 1)

        barrier()

        wr = lambda i: wmid[0:1, D * i:D * (i + 1)]
        def body(sync):
            sync.wait_ge(s_v.h, s_v.count)
            ins = sync.dma_start(out=wr(0), in_=bass.AP(b2_h, 0, [[D, 1], [1, D]]))
            s_wb2.dma(ins)
        block.sync(body)
        b2_loaded = s_wb2.count
        emit_quant(wr(0), wr(1), wr(2), wr(3), wr(4), "fp16",
                   gates=[(s_wb2, b2_loaded), (s_mm, s_mm.count)], rows=1)

        def body(s_):
            s_.wait_ge(s_v.h, s_v.count)
            ins = s_.dma_start(out=sc_row.ap()[1:2, 0:D], in_=wr(1))
            s_row1.dma(ins)
        block.scalar(body)

        barrier()

        w1h = wba[:, 0:16384]
        w1l = wbb[:, 0:16384]
        qd = {0: 0, 1: 0}
        qg = {0: 0, 1: 0}
        for half in range(2):
            for hcl in range(16):
                hc = 16 * half + hcl
                sw = s_w1[hcl % 2]

                def body(sync, hc=hc, hcl=hcl, half=half, sw=sw):
                    if hcl >= 2:
                        sync.wait_ge(s_mm.h, ps[f"w1t_{hcl - 2}"])
                    elif half == 1:
                        sync.wait_ge(s_mm.h, ps["ffn1_mm"])
                    ins = sync.dma_start(out=w1st[hcl % 2],
                                         in_=w1_h.ap()[128 * hc:128 * (hc + 1), :])
                    sw.dma(ins)
                block.sync(body)
                w1_loaded = sw.count

                tb4 = []
                def body(te, hcl=hcl, sw=sw, w1_loaded=w1_loaded):
                    te.wait_ge(sw.h, w1_loaded)
                    for hh in range(2):
                        b = get_bank(te)
                        tb4.append(b)
                        for cc in range(4):
                            c = 4 * hh + cc
                            ins = te.transpose(PB[b][:, 128 * cc:128 * (cc + 1)],
                                               w1st[hcl % 2][:, 128 * c:128 * (c + 1)], ident[:])
                        s_mm.inc(ins)
                block.tensor(body)
                ps[f"w1t_{hcl}"] = s_mm.count
                tdone = s_mm.count

                def body(v, hcl=hcl, tdone=tdone):
                    v.wait_ge(s_mm.h, tdone)
                    for hh in range(2):
                        off = 8192 * hh + 128 * hcl
                        dh = bass.AP(wba, off, [[8 * H2, 128], [2048, 4], [1, 128]])
                        dl = bass.AP(wbb, off, [[8 * H2, 128], [2048, 4], [1, 128]])
                        v.tensor_copy(dh, PB[tb4[hh]][:])
                        v.tensor_tensor(dl, PB[tb4[hh]][:], dh, Alu.subtract)
                VB(body)
                for hh in range(2):
                    bank_done(tb4[hh], s_v)
            w1t_ready_v = s_v.count

            def d1_x2pipe(grp, half=half):
                sg, gl = divmod(grp, 4)
                gp = grp % 2
                for itl in range(4):
                    ti = 4 * gl + itl
                    sxl = s_xd1 if itl % 2 == 0 else s_xd1b

                    def body(sync, sg=sg, ti=ti, itl=itl, grp=grp, half=half,
                             gp=gp, sxl=sxl):
                        if grp == 0 and half == 0 and itl == 0:
                            sync.wait_ge(s_v.h, w1t_ready_v)
                        if itl == 0:
                            sync.wait_ge(s_mm.h, ps.get(f"ffn1_{gp}", 0))
                            if half > 0 and grp == 0:
                                sync.wait_ge(s_mm.h, ps["ffn1_mm"])
                        sync.wait_ge(s_mm.h, ps.get(f"d1_xt_{itl % 2}", 0))
                        ins = sync.dma_start(out=xmt_d1[itl % 2],
                                             in_=sc_x2.ap()[sg, 128 * ti:128 * (ti + 1), :])
                        sxl.dma(ins)
                    block.sync(body)
                    x2_loaded = sxl.count

                    tb4 = []
                    def body(te, itl=itl, sxl=sxl, x2_loaded=x2_loaded):
                        te.wait_ge(sxl.h, x2_loaded)
                        for hh in range(2):
                            b = get_bank(te)
                            tb4.append(b)
                            for cc in range(4):
                                c = 4 * hh + cc
                                ins = te.transpose(PB[b][:, 128 * cc:128 * (cc + 1)],
                                                   xmt_d1[itl % 2][:, 128 * c:128 * (c + 1)],
                                                   ident[:])
                            s_mm.inc(ins)
                    block.tensor(body)
                    ps[f"d1_xt_{itl % 2}"] = s_mm.count
                    tdone = s_mm.count

                    def body(v, itl=itl, gp=gp, tdone=tdone):
                        v.wait_ge(s_mm.h, tdone)
                        if itl == 0:
                            v.wait_ge(s_xb[gp].h, ps.get(f"xbs_{gp}", 0))
                        for hh in range(2):
                            off = 16384 + 4096 * gp + 2048 * hh + 128 * itl
                            dh = bass.AP(wba, off, [[8 * H2, 128], [512, 4], [1, 128]])
                            dl = bass.AP(wbb, off, [[8 * H2, 128], [512, 4], [1, 128]])
                            v.tensor_copy(dh, PB[tb4[hh]][:])
                            v.tensor_tensor(dl, PB[tb4[hh]][:], dh, Alu.subtract)
                    VB(body)
                    for hh in range(2):
                        bank_done(tb4[hh], s_v)
                if half == 0:
                    def body(g, grp=grp, gp=gp, evdone=s_v.count):
                        g.wait_ge(s_v.h, evdone)
                        for hl, arena in ((0, x2h[gp]), (1, x2l[gp])):
                            ins = g.dma_start(out=sc_xb.ap()[grp, hl, :, :],
                                              in_=arena)
                            s_xb[gp].dma(ins)
                    block.gpsimd(body)
                    ps[f"xbs_{gp}"] = s_xb[gp].count
                return (s_v, s_v.count)

            def d1_x2load(grp):
                gp = grp % 2
                sxl = s_xd1 if gp == 0 else s_xd1b

                def body(sync, grp=grp, gp=gp, sxl=sxl):
                    sync.wait_ge(s_mm.h, ps.get(f"ffn1_{gp}", 0))
                    if grp <= 1:
                        sync.wait_ge(s_mm.h, ps["ffn1_mm"])
                        sync.wait_ge(s_xb[0].h, s_xb[0].count)
                        sync.wait_ge(s_xb[1].h, s_xb[1].count)
                    for hl, arena in ((0, x2h[gp]), (1, x2l[gp])):
                        ins = sync.dma_start(out=arena, in_=sc_xb.ap()[grp, hl, :, :])
                        sxl.dma(ins)
                block.sync(body)
                return (sxl, sxl.count)

            if half == 0:
                x2r = {0: d1_x2pipe(0)}
            else:
                x2r = {0: d1_x2load(0)}
            for grp in range(8):
                sg, gl = divmod(grp, 4)
                gp = grp % 2
                x2fm_ready = x2r[grp]

                for hcp in range(8):
                    e = hcp % 2

                    for sub in range(2):
                        hc = 16 * half + 2 * hcp + sub
                        hcl = 2 * hcp + sub

                        def body(te, hcl=hcl, gp=gp, x2fm_ready=x2fm_ready):
                            te.wait_ge(x2fm_ready[0].h, x2fm_ready[1])
                            b = get_bank(te)
                            tbanks.append(b)
                            for c in range(8):
                                ws = slice(c * 2048 + 128 * hcl, c * 2048 + 128 * (hcl + 1))
                                xs = slice(512 * c, 512 * (c + 1))
                                te.matmul(PB[b][:], w1h[:, ws], x2h[gp][:, xs],
                                          start=(c == 0), stop=False)
                                te.matmul(PB[b][:], w1h[:, ws], x2l[gp][:, xs],
                                          start=False, stop=False)
                                ins = te.matmul(PB[b][:], w1l[:, ws], x2h[gp][:, xs],
                                                start=False, stop=(c == 7))
                            s_mm.inc(ins)
                        block.tensor(body)
                        fbank = tbanks[-1]
                        this_mm = s_mm.count

                        def body(s_, hc=hc, sub=sub, grp=grp, half=half, e=e,
                                 fbank=fbank, this_mm=this_mm, evfree=qd[e]):
                            s_.wait_ge(s_mm.h, this_mm)
                            if grp == 0 and half == 0 and hc < 2:
                                s_.wait_ge(s_v.h, lnbc_ready)
                            elif sub == 0:
                                s_.wait_ge(s_v.h, evfree)
                            s_.activation(qsrc[e][:, 512 * sub:512 * (sub + 1)],
                                          PB[fbank][:], Act.Relu, bias=b1tab[:, hc:hc + 1])
                        SCB(body)
                        bank_done(fbank, s_s)

                    qd[e] = emit_quant(
                        qsrc[e][:], hqb[e][:], qd_t0, qd_t12[e][0],
                        qd_t12[e][1], "e4m3", unsigned=True, sub_scalar=None,
                        gates=[(s_s, s_s.count)],
                        dst_gates=[(s_hq[e], s_hq[e].count)])
                    qdd = s_v.count

                    def body(g, gl=gl, sg=sg, half=half, hcp=hcp, e=e, qdone=qdd):
                        g.wait_ge(s_v.h, qdone)
                        tt0 = 16 * sg + 4 * gl
                        for sub in range(2):
                            hc = 16 * half + 2 * hcp + sub
                            for itl in range(4):
                                ins = g.dma_start(
                                    out=sc_hq.ap()[tt0 + itl, hc, :, :],
                                    in_=hqb[e][:, 512 * sub + 128 * itl:512 * sub + 128 * (itl + 1)])
                                s_hq[e].dma(ins)
                    block.gpsimd(body)
                    if hcp == 3 and grp + 1 < 8:
                        x2r[grp + 1] = (d1_x2pipe(grp + 1) if half == 0
                                        else d1_x2load(grp + 1))
                ps["ffn1_mm"] = ps[f"ffn1_{gp}"] = s_mm.count
                tbanks.clear()

        barrier()

        def body(sync):
            for hc in range(32):
                ins = sync.dma_start_transpose(
                    out=wba[:, 1024 * hc:1024 * (hc + 1)],
                    in_=bass.AP(sc_wq2, 128 * hc, [[HF, D], [1, 128]]))
                s_wq.dma(ins)
            ins = sync.dma_start(out=bqt[:], in_=bass.AP(sc_row, H2, [[0, 128], [1, D]]))
            s_wq.dma(ins)
        block.sync(body)
        wq_loaded = s_wq.count

        s_d2db = [s_d2, s_vb[0]]
        s_outdb = [s_out, s_vb[1]]
        d2_pend = []

        def d2_ln_tail():
            tp, sp, tip, dbp, obufp, sdop, scnt = d2_pend.pop(0)

            def body(v, obufp=obufp, scnt=scnt):
                v.wait_ge(s_s.h, scnt)
                v.tensor_scalar(obufp, obufp, st4[:], None, Alu.mult)
            VB(body)
            ps[f"d2_out_{dbp}"] = s_v.count

            def body(s_, sp=sp, tip=tip, obufp=obufp, sdop=sdop):
                s_.wait_ge(s_v.h, s_v.count)
                ins = s_.dma_start(out=out_h.ap()[sp, 128 * tip:128 * (tip + 1), :],
                                   in_=obufp)
                sdop.dma(ins)
            block.scalar(body)

        for t in range(TT):
            s, ti = divmod(t, NT)
            db = t % 2
            sd = s_d2db[db]
            sdo = s_outdb[db]
            hql = hqld[db]
            if t >= 1:
                d2_ln_tail()
            obuf = obuf_db[db]
            oqbuf = oqbuf_db[db]
            u3buf = u3_db[db]

            def body(sync, t=t, s=s, ti=ti, db=db, sd=sd, hql=hql, u3buf=u3buf):
                if t == 0:
                    sync.wait_ge(s_wq.h, wq_loaded)
                if t >= 2:
                    sync.wait_ge(s_mm.h, ps[f"d2_mm_{db}"])
                    sync.wait_ge(s_v.h, ps.get(f"d2_out_{db}", 0))
                sync.dma_start(out=hql.rearrange("p (a b) -> p a b", b=128),
                               in_=bass.AP(sc_hq, t * 32 * 128 * 128,
                                           [[128, 128], [128 * 128, 32], [1, 128]])).then_inc(sd.h, 16)
                ins = sync.dma_start(out=u3buf, in_=sc_x2.ap()[s, 128 * ti:128 * (ti + 1), :])
                sd.dma(ins)
            block.sync(body)
            sd.count += 16
            d2_loaded = sd.count

            ob2 = []
            def body(te, sd=sd, hql=hql, d2_loaded=d2_loaded):
                te.wait_ge(sd.h, d2_loaded)
                for b2 in range(2):
                    b = get_bank(te)
                    ob2.append(b)
                    for hc in range(32):
                        ins = te.matmul(PB[b][:], hql[:, 128 * hc:128 * (hc + 1)],
                                        wba[:, 1024 * hc + 512 * b2:1024 * hc + 512 * (b2 + 1)],
                                        start=(hc == 0), stop=(hc == 31))
                    s_mm.inc(ins)
            block.tensor(body)
            ps[f"d2_mm_{db}"] = s_mm.count
            mm2 = s_mm.count

            def body(v, mm2=mm2, t=t, obuf=obuf, sdo=sdo):
                v.wait_ge(s_mm.h, mm2)
                if t >= 2:
                    v.wait_ge(sdo.h, 16 * (t // 2))
                for b2 in range(2):
                    v.tensor_tensor(obuf[:, 512 * b2:512 * (b2 + 1)], PB[ob2[b2]][:],
                                    bqt[:, 512 * b2:512 * (b2 + 1)], Alu.add)
            VB(body)
            for b2 in range(2):
                bank_done(ob2[b2], s_v)
            ob2.clear()

            emit_quant(obuf, oqbuf, q2_t0, q2_t12[db][0], q2_t12[db][1], "fp16")

            def body(v, sd=sd, u3buf=u3buf, oqbuf=oqbuf, d2_loaded=d2_loaded,
                     pscnt=ps.get("d2_scnt", 0)):
                v.wait_ge(sd.h, d2_loaded)
                v.wait_ge(s_s.h, pscnt)
                v.tensor_tensor(u3buf, u3buf, oqbuf, Alu.add)
                v.reduce_sum(st1[:], u3buf, axis=AX.X)
                v.tensor_scalar(st2[:], st1[:], 1.0 / D, None, Alu.mult)
                v.tensor_scalar(obuf, u3buf, st2[:], None, Alu.subtract)
                v.scalar_tensor_tensor(lnsq_d2, obuf, 1.0, obuf, Alu.mult, Alu.mult,
                                       accum_out=st3[:])
            VB(body)
            ps[f"d2_x2_{db}"] = s_v.count

            def body(s_):
                s_.wait_ge(s_v.h, s_v.count)
                s_.activation(st4[:], st3[:], Act.Ln, bias=eps_t[:], scale=1.0 / D)
                s_.activation(st4[:], st4[:], Act.Exp, scale=-0.5)
            SCB(body)
            ps["d2_scnt"] = s_s.count
            d2_pend.append((t, s, ti, db, obuf, sdo, s_s.count))
        while d2_pend:
            d2_ln_tail()

        def body(sync):
            for sm in ALL_SEMS:
                if sm.count > 0:
                    sync.wait_ge(sm.h, sm.count)
        block.sync(body)

    nc._keepalive = es
    return nc


_NC_CACHE = {}


def kernel(**inputs):
    if "prog" not in _NC_CACHE:
        _NC_CACHE["prog"] = build_program()
    nc = _NC_CACHE["prog"]

    x = np.ascontiguousarray(inputs["L0_attn_x"], dtype=np.float32)
    weights = {k: np.ascontiguousarray(v, dtype=np.float32) for k, v in inputs.items()
               if k != "L0_attn_x"}
    in_maps = []
    for c in range(NCORES):
        m = {"x": np.ascontiguousarray(x[SEQ * c:SEQ * (c + 1)])}
        m.update(weights)
        in_maps.append(m)
    res = run_bass_kernel_spmd(nc, in_maps, core_ids=list(range(NCORES)))
    out = np.concatenate([np.asarray(r["out"]) for r in res.results], axis=0)
    return out.astype(np.float32)
